# version 08
# baseline (speedup 1.0000x reference)
import contextlib
import numpy as np
import ml_dtypes
import concourse.bass as bass
import concourse.mybir as mybir
from concourse.bass_utils import run_bass_kernel_spmd

BFNP = ml_dtypes.bfloat16
F32 = mybir.dt.float32
BF16 = mybir.dt.bfloat16
AF = mybir.ActivationFunctionType
ALU = mybir.AluOpType
AX = mybir.AxisListType

D = 2048
DC = 16
T = 1024
C = 3072
CT = 24
OT0 = 16
H = 16
F = 8192
FC = 64
J = 20
QB = 2
NCORES = 8
SCALE = 1.0 / float(np.sqrt(np.float32(128.0)))
LN_EPS = 1e-5
RMS_EPS = 1e-8
PAD_BIAS = -88.0

def xnf_col(c, t=0):   return c * 3072 + t * 128
def wqtr_col(buf, k=0): return 49152 + buf * 8192 + k * 512
def p_col(hc, q=0):    return hc * 1024 + q
def ctx_col(h, q=0):   return h * 1024 + q
def kbuf_col(gb, hl):  return 16384 + gb * 24576 + hl * 3072
def vbuf_col(gb, tt=0): return 16384 + gb * 24576 + 12288 + tt * 512
def wo_col(k):         return 16384 + k * 2048

def xslot_col(s):      return s * 2048
SCR_A = 4096
SCR_B = 5120
P3_G = 4096
P4_SG = 4096

def xnslot_col(s):     return s * 2048
def pwslot_col(s):     return 4096 + s * 2048
def wstage_col(s):     return s * 2048
def wdslice_col(s):    return s * 512
MASKS_COL = 0
def ptil_col(s):       return 4096 + s * 512
def qring_col(s):      return 6144 + s * 512

ENGS = ("T", "V", "S", "Y", "G")
NSLOT = 12
DMA_SEMS = [f"{q}{i}" for q in ("Y", "S", "G") for i in range(NSLOT)]
ALL_SEMS = ["T", "V", "S"] + DMA_SEMS


class Prog:

    def __init__(self, nc):
        self.nc = nc
        self.thunks = {e: [] for e in ENGS}
        self.cnt = {s: 0 for s in ALL_SEMS}
        self.qrr = {"Y": 0, "S": 0, "G": 0}
        self.water = {}
        self.res = {}

    def _deps_read(self, iv):
        arena, lo, hi = iv
        return [r["w"] for r in self.res.get(arena, [])
                if r["lo"] < hi and lo < r["hi"] and r["w"] is not None]

    def _deps_write(self, iv):
        arena, lo, hi = iv
        deps, keep = [], []
        for r in self.res.get(arena, []):
            if r["lo"] < hi and lo < r["hi"]:
                if r["w"] is not None:
                    deps.append(r["w"])
                deps.extend(r["r"])
                if r["lo"] < lo:
                    keep.append({"lo": r["lo"], "hi": lo, "w": r["w"], "r": list(r["r"])})
                if r["hi"] > hi:
                    keep.append({"lo": hi, "hi": r["hi"], "w": r["w"], "r": list(r["r"])})
            else:
                keep.append(r)
        self.res[arena] = keep
        return deps

    def _note_read(self, iv, tok):
        arena, lo, hi = iv
        for r in self.res.get(arena, []):
            if r["lo"] < hi and lo < r["hi"]:
                r["r"].append(tok)

    def op(self, eng, fn, reads=(), writes=(), dma=False):
        if dma:
            slot = self.qrr[eng]
            self.qrr[eng] = (slot + 1) % NSLOT
            sem = f"{eng}{slot}"
            step = 16
        else:
            sem = eng
            step = 1
        needs = {}
        for iv in reads:
            for (s, v) in self._deps_read(iv):
                if needs.get(s, 0) < v:
                    needs[s] = v
        for iv in writes:
            for (s, v) in self._deps_write(iv):
                if needs.get(s, 0) < v:
                    needs[s] = v
        if dma and self.cnt[sem] > 0 and needs.get(sem, 0) < self.cnt[sem]:
            needs[sem] = self.cnt[sem]
        waits = []
        for s, v in needs.items():
            if self.water.get((eng, s), 0) < v:
                self.water[(eng, s)] = v
                waits.append((s, v))
        self.cnt[sem] += step
        tok = (sem, self.cnt[sem])
        for iv in writes:
            arena, lo, hi = iv
            self.res.setdefault(arena, []).append(
                {"lo": lo, "hi": hi, "w": tok, "r": []})
        for iv in reads:
            self._note_read(iv, tok)

        def thunk(e, sems, waits=waits, fn=fn, sem=sem, step=step):
            for (s, v) in waits:
                e.wait_ge(sems[s], v)
            fn(e).then_inc(sems[sem], step)
        self.thunks[eng].append(thunk)

    def barrier_wait(self, eng, sem_targets):
        def thunk(e, sems, w=tuple(sem_targets)):
            for (s, v) in w:
                e.wait_ge(sems[s], v)
        self.thunks[eng].append(thunk)


def build_program():
    nc = bass.Bass("TRN2", target_bir_lowering=False)
    g = Prog(nc)

    x_tm = nc.dram_tensor("x_tm", [C, D], F32, kind="ExternalInput").ap()
    wq = nc.dram_tensor("wq", [D, D], F32, kind="ExternalInput").ap()
    wk = nc.dram_tensor("wk", [D, D], F32, kind="ExternalInput").ap()
    wv = nc.dram_tensor("wv", [D, D], F32, kind="ExternalInput").ap()
    wo = nc.dram_tensor("wo", [D, D], F32, kind="ExternalInput").ap()
    wgT = nc.dram_tensor("wgT", [D, F], F32, kind="ExternalInput").ap()
    wuT = nc.dram_tensor("wuT", [D, F], F32, kind="ExternalInput").ap()
    wdT = nc.dram_tensor("wdT", [F, D], F32, kind="ExternalInput").ap()
    masks_in = nc.dram_tensor("masks8", [128, 8 * 512], BF16, kind="ExternalInput").ap()
    padbias_in = nc.dram_tensor("padbias", [128, 2 * J], F32, kind="ExternalInput").ap()
    ident_in = nc.dram_tensor("ident", [128, 128], BF16, kind="ExternalInput").ap()
    onesc_in = nc.dram_tensor("onesc", [128, 1], BF16, kind="ExternalInput").ap()
    onesr_in = nc.dram_tensor("onesr", [1, 128], F32, kind="ExternalInput").ap()
    out_hbm = nc.dram_tensor("out", [T, D], F32, kind="ExternalOutput").ap()

    d_kf = nc.dram_tensor("d_kf", [H, 6, 128, 512], BF16, kind="Internal").ap()
    d_vtm = nc.dram_tensor("d_vtm", [4, CT, 128, 512], BF16, kind="Internal").ap()
    d_qf = nc.dram_tensor("d_qf", [H, QB, 128, 512], BF16, kind="Internal").ap()
    d_g = nc.dram_tensor("d_g", [8, 128, D], F32, kind="Internal").ap()
    d_wgs = nc.dram_tensor("d_wgs", [FC, 128, 2048], BF16, kind="Internal").ap()
    d_wus = nc.dram_tensor("d_wus", [FC, 128, 2048], BF16, kind="Internal").ap()
    d_wds = nc.dram_tensor("d_wds", [FC, 128, 2048], BF16, kind="Internal").ap()

    with contextlib.ExitStack() as stk:
        def S_(name, shape, dt):
            return stk.enter_context(nc.sbuf_tensor(name, shape, dt))
        AR1 = S_("AR1", [128, 65536], BF16)
        AR2f = S_("AR2f", [128, 6144], F32)
        AR2b = S_("AR2b", [128, 8192], BF16)
        XQ = S_("XQ", [128, DC * T], BF16)
        ident = S_("identt", [128, 128], BF16)
        onesc = S_("onesct", [128, 1], BF16)
        onesr = S_("onesrt", [1, 128], F32)
        padb = S_("padbt", [128, 2 * J], F32)
        bn6 = S_("bn6", [128, 2 * 24], F32)
        bn2 = S_("bn2", [128, 2 * 2], F32)
        cols = S_("cols", [128, 2 * 8], F32)
        linv = S_("linv", [1, 512], F32)
        linvb = S_("linvb", [128, 512], F32)
        EVN = 2
        evac = S_("evac", [128, EVN * 512], BF16)

        PS = [stk.enter_context(nc.psum_tensor(f"ps{i}", [128, 512], F32))
              for i in range(6)]
        PT = [stk.enter_context(nc.psum_tensor(f"pt{i}", [128, 128], BF16))
              for i in range(2)]

        sems = {}
        sem_ctx = contextlib.ExitStack()
        for s in ALL_SEMS:
            sems[s] = sem_ctx.enter_context(nc.semaphore("sem" + s))
        stk.enter_context(sem_ctx)
        block = stk.enter_context(nc.Block())

        A1, A2F, A2B, AXQ = "AR1", "AR2f", "AR2b", "XQ"
        PSR = "psum"
        DKF, DVT, DQF, DG = "d_kf", "d_vtm", "d_qf", "d_g"
        DWG, DWU, DWD = "d_wgs", "d_wus", "d_wds"
        CST = "cst"
        CST_IDS = {"ident": 0, "onesc": 1, "onesr": 2, "padb": 3,
                   "linv": 4, "linvb": 5}
        def cst(n):
            i = CST_IDS[n]
            return (CST, i, i + 1)
        def bank(i):
            return (PSR, i, i + 1)
        def stv(ring, field):
            i = ring * 16 + field
            return ("st", i, i + 1)

        g.op("Y", lambda e: e.dma_start(out=ident[:], in_=ident_in),
             writes=[cst("ident")], dma=True)
        g.op("Y", lambda e: e.dma_start(out=onesc[:], in_=onesc_in),
             writes=[cst("onesc")], dma=True)
        g.op("Y", lambda e: e.dma_start(out=onesr[:], in_=onesr_in),
             writes=[cst("onesr")], dma=True)
        g.op("Y", lambda e: e.dma_start(out=padb[:], in_=padbias_in),
             writes=[cst("padb")], dma=True)

        def emit_pw_chunk(src_ap, dst_ap, dst_arena, idx, slot, colblock):
            lo = pwslot_col(slot)
            if colblock:
                for q4 in range(4):
                    def fn(e, src_ap=src_ap, idx=idx, lo=lo, q4=q4):
                        src = src_ap[q4 * 512:(q4 + 1) * 512,
                                     idx * 128:(idx + 1) * 128]
                        src = src.rearrange("(k p) h -> p k h", p=128)
                        dst = AR2b[:, lo + q4 * 512: lo + (q4 + 1) * 512]
                        dst = dst.rearrange("p (k h) -> p k h", k=4, h=128)
                        return e.dma_start(out=dst, in_=src)
                    g.op("G", fn, writes=[(A2B, lo + q4 * 512, lo + (q4 + 1) * 512)],
                         dma=True)
            else:
                def fn(e, src_ap=src_ap, idx=idx, lo=lo):
                    return e.dma_start(out=AR2b[:, lo:lo + 2048],
                                       in_=src_ap[idx * 128:(idx + 1) * 128, :])
                g.op("G", fn, writes=[(A2B, lo, lo + 2048)], dma=True)
            def fns(e, lo=lo):
                return e.sign(AR2b[:, lo:lo + 2048], AR2b[:, lo:lo + 2048])
            g.op("S", fns, reads=[(A2B, lo, lo + 2048)],
                 writes=[(A2B, lo, lo + 2048)])
            def fnw(e, dst_ap=dst_ap, idx=idx, lo=lo):
                return e.dma_start(out=dst_ap[idx], in_=AR2b[:, lo:lo + 2048])
            g.op("S", fnw, reads=[(A2B, lo, lo + 2048)],
                 writes=[(dst_arena, idx, idx + 1)], dma=True)

        pw_jobs = ([(wgT, d_wgs, DWG, i, True) for i in range(FC)]
                   + [(wuT, d_wus, DWU, i, True) for i in range(FC)]
                   + [(wdT, d_wds, DWD, i, False) for i in range(FC)])
        pw_state = {"pos": 0, "slot": 0}
        def emit_pw(n):
            for _ in range(n):
                if pw_state["pos"] >= len(pw_jobs):
                    return
                src_ap, dst_ap, arena, idx, colblock = pw_jobs[pw_state["pos"]]
                emit_pw_chunk(src_ap, dst_ap, arena, idx, pw_state["slot"], colblock)
                pw_state["slot"] ^= 1
                pw_state["pos"] += 1

        def emit_qkvo_quarter_load(w_ap, a, buf):
            def fn(e, w_ap=w_ap, a=a, buf=buf):
                src = w_ap[:, a * 512:(a + 1) * 512].rearrange(
                    "(k p) n -> p k n", p=128)
                dst = AR1[:, wqtr_col(buf, 0):wqtr_col(buf, 0) + 8192].rearrange(
                    "p (k n) -> p k n", k=DC)
                return e.dma_start(out=dst, in_=src)
            g.op("G", fn, writes=[(A1, wqtr_col(buf, 0), wqtr_col(buf, 0) + 8192)],
                 dma=True)

        emit_qkvo_quarter_load(wk, 0, 0)
        emit_qkvo_quarter_load(wk, 1, 1)


        def emit_kproj_block(tb, quarters, bufmap):
            for a in quarters:
                buf = bufmap[a]
                for mi in range(4):
                    h = a * 4 + mi
                    pbk = (h * 6 + tb) % 4
                    for k in range(DC - 1, -1, -1):
                        def fmm(e, buf=buf, mi=mi, k=k, tb=tb, pbk=pbk):
                            lhs = AR1[:, wqtr_col(buf, k) + mi * 128:
                                      wqtr_col(buf, k) + (mi + 1) * 128]
                            rhs = AR1[:, xnf_col(k) + tb * 512:
                                      xnf_col(k) + (tb + 1) * 512]
                            return e.matmul(PS[pbk][:], lhs, rhs,
                                            start=(k == DC - 1), stop=(k == 0))
                        g.op("T", fmm,
                             reads=[(A1, wqtr_col(buf, k), wqtr_col(buf, k) + 512),
                                    (A1, xnf_col(k) + tb * 512,
                                     xnf_col(k) + (tb + 1) * 512)],
                             writes=[bank(pbk)])
                    ev = (h * 6 + tb) % EVN
                    def fevk(e, pbk=pbk, ev=ev):
                        return e.tensor_copy(evac[:, ev * 512:(ev + 1) * 512],
                                             PS[pbk][:])
                    g.op("V", fevk, reads=[bank(pbk)], writes=[("ev", ev, ev + 1)])
                    def fspk(e, h=h, tb=tb, ev=ev):
                        return e.dma_start(out=d_kf[h, tb],
                                           in_=evac[:, ev * 512:(ev + 1) * 512])
                    g.op("Y", fspk, reads=[("ev", ev, ev + 1)],
                         writes=[(DKF, h * 6 + tb, h * 6 + tb + 1)], dma=True)

        for t in range(CT):
            xs = t % 2
            xlo = xslot_col(xs)
            rg = t % 2
            b6lo = rg * 24
            b2lo = rg * 2
            clo = rg * 8
            def fxl(e, t=t, xlo=xlo):
                return e.dma_start(out=AR2f[:, xlo:xlo + 2048],
                                  in_=x_tm[t * 128:(t + 1) * 128, :])
            g.op("Y", fxl, writes=[(A2F, xlo, xlo + 2048)], dma=True)
            for c4 in range(4):
                def fbs(e, xlo=xlo, b6lo=b6lo, c4=c4):
                    return e.bn_stats(bn6[:, b6lo + c4 * 6:b6lo + (c4 + 1) * 6],
                                      AR2f[:, xlo + c4 * 512: xlo + (c4 + 1) * 512])
                g.op("V", fbs, reads=[(A2F, xlo, xlo + 2048)],
                     writes=[stv(rg, c4)])
            def fag(e, b6lo=b6lo, b2lo=b2lo):
                return e.bn_aggr(bn2[:, b2lo:b2lo + 2], bn6[:, b6lo:b6lo + 24])
            g.op("V", fag, reads=[stv(rg, 0), stv(rg, 1), stv(rg, 2), stv(rg, 3)],
                 writes=[stv(rg, 4)])
            def fep(e, b2lo=b2lo, clo=clo):
                return e.tensor_scalar(cols[:, clo + 2:clo + 3],
                                       bn2[:, b2lo + 1:b2lo + 2],
                                       LN_EPS, None, ALU.add)
            g.op("V", fep, reads=[stv(rg, 4)], writes=[stv(rg, 5)])
            def fsq(e, clo=clo):
                return e.activation(cols[:, clo + 2:clo + 3],
                                    cols[:, clo + 2:clo + 3], AF.Sqrt)
            g.op("S", fsq, reads=[stv(rg, 5)], writes=[stv(rg, 5)])
            def frc(e, clo=clo):
                return e.reciprocal(cols[:, clo:clo + 1], cols[:, clo + 2:clo + 3])
            g.op("V", frc, reads=[stv(rg, 5)], writes=[stv(rg, 6)])
            def fbt(e, b2lo=b2lo, clo=clo):
                return e.tensor_scalar(cols[:, clo + 1:clo + 2],
                                       bn2[:, b2lo:b2lo + 1],
                                       cols[:, clo:clo + 1], -1.0,
                                       ALU.mult, ALU.mult)
            g.op("V", fbt, reads=[stv(rg, 4), stv(rg, 6)], writes=[stv(rg, 7)])
            ns = t % 2
            nlo = xnslot_col(ns)
            def fap(e, xlo=xlo, nlo=nlo, clo=clo):
                return e.activation(AR2b[:, nlo:nlo + 2048],
                                    AR2f[:, xlo:xlo + 2048], AF.Identity,
                                    bias=cols[:, clo + 1:clo + 2],
                                    scale=cols[:, clo:clo + 1])
            g.op("S", fap, reads=[(A2F, xlo, xlo + 2048), stv(rg, 6), stv(rg, 7)],
                 writes=[(A2B, nlo, nlo + 2048)])
            for c in range(DC):
                pb = c % 2
                def ftr(e, nlo=nlo, c=c, pb=pb):
                    return e.transpose(PT[pb][:],
                                       AR2b[:, nlo + c * 128: nlo + (c + 1) * 128],
                                       ident[:])
                g.op("T", ftr, reads=[(A2B, nlo, nlo + 2048), cst("ident")],
                     writes=[bank(6 + pb)])
                dst = xnf_col(c, t)
                def fev(e, pb=pb, dst=dst):
                    return e.copy(AR1[:, dst:dst + 128], PT[pb][:])
                g.op("S", fev, reads=[bank(6 + pb)], writes=[(A1, dst, dst + 128)])

            if t >= OT0:
                tq = t - OT0
                for c4 in range(4):
                    def fbs2(e, nlo=nlo, b6lo=b6lo, c4=c4):
                        return e.bn_stats(bn6[:, b6lo + c4 * 6:b6lo + (c4 + 1) * 6],
                                          AR2b[:, nlo + c4 * 512: nlo + (c4 + 1) * 512])
                    g.op("V", fbs2, reads=[(A2B, nlo, nlo + 2048)],
                         writes=[stv(rg, c4)])
                def fag2(e, b6lo=b6lo, b2lo=b2lo):
                    return e.bn_aggr(bn2[:, b2lo:b2lo + 2], bn6[:, b6lo:b6lo + 24])
                g.op("V", fag2, reads=[stv(rg, 0), stv(rg, 1), stv(rg, 2), stv(rg, 3)],
                     writes=[stv(rg, 4)])
                def fep2(e, b2lo=b2lo, clo=clo):
                    return e.tensor_scalar(cols[:, clo + 2:clo + 3],
                                           bn2[:, b2lo + 1:b2lo + 2],
                                           RMS_EPS, None, ALU.add)
                g.op("V", fep2, reads=[stv(rg, 4)], writes=[stv(rg, 5)])
                def fsq2(e, clo=clo):
                    return e.activation(cols[:, clo + 2:clo + 3],
                                        cols[:, clo + 2:clo + 3], AF.Sqrt)
                g.op("S", fsq2, reads=[stv(rg, 5)], writes=[stv(rg, 5)])
                def frc2(e, clo=clo):
                    return e.reciprocal(cols[:, clo + 4:clo + 5],
                                        cols[:, clo + 2:clo + 3])
                g.op("V", frc2, reads=[stv(rg, 5)], writes=[stv(rg, 8)])
                def ftt(e, nlo=nlo, xlo=xlo, b2lo=b2lo):
                    return e.tensor_scalar(AR2f[:, xlo:xlo + 2048],
                                           AR2b[:, nlo:nlo + 2048],
                                           bn2[:, b2lo:b2lo + 1], None,
                                           ALU.subtract)
                g.op("V", ftt, reads=[(A2B, nlo, nlo + 2048), stv(rg, 4)],
                     writes=[(A2F, xlo, xlo + 2048)])
                def fam(e, xlo=xlo, clo=clo):
                    return e.reduce_max(cols[:, clo + 5:clo + 6],
                                        AR2f[:, xlo:xlo + 2048],
                                        axis=AX.X, apply_absolute_value=True)
                g.op("V", fam, reads=[(A2F, xlo, xlo + 2048)], writes=[stv(rg, 9)])
                def fc1(e, clo=clo):
                    return e.reciprocal(cols[:, clo + 6:clo + 7],
                                        cols[:, clo + 5:clo + 6])
                g.op("V", fc1, reads=[stv(rg, 9)], writes=[stv(rg, 10)])
                def fc1b(e, clo=clo):
                    return e.tensor_scalar(cols[:, clo + 6:clo + 7],
                                           cols[:, clo + 6:clo + 7],
                                           127.0, None, ALU.mult)
                g.op("V", fc1b, reads=[stv(rg, 10)], writes=[stv(rg, 10)])
                def fiv(e, clo=clo):
                    return e.tensor_scalar(cols[:, clo + 7:clo + 8],
                                           cols[:, clo + 5:clo + 6],
                                           cols[:, clo + 4:clo + 5], 1.0 / 127.0,
                                           ALU.mult, ALU.mult)
                g.op("V", fiv, reads=[stv(rg, 9), stv(rg, 8)], writes=[stv(rg, 11)])
                for hf in range(2):
                    tlo = xlo + hf * 1024
                    def fz(e, tlo=tlo, clo=clo):
                        return e.tensor_scalar(AR2f[:, SCR_A:SCR_A + 1024],
                                               AR2f[:, tlo:tlo + 1024],
                                               cols[:, clo + 6:clo + 7], 128.5,
                                               ALU.mult, ALU.add)
                    g.op("V", fz, reads=[(A2F, tlo, tlo + 1024), stv(rg, 10)],
                         writes=[(A2F, SCR_A, SCR_A + 1024)])
                    for kb in range(7, -1, -1):
                        def fb(e, kb=kb):
                            return e.tensor_scalar(AR2f[:, SCR_B:SCR_B + 1024],
                                                   AR2f[:, SCR_A:SCR_A + 1024],
                                                   float(2 ** kb), -float(2 ** kb),
                                                   ALU.is_ge, ALU.mult)
                        g.op("V", fb, reads=[(A2F, SCR_A, SCR_A + 1024)],
                             writes=[(A2F, SCR_B, SCR_B + 1024)])
                        def fs(e):
                            return e.tensor_tensor(AR2f[:, SCR_A:SCR_A + 1024],
                                                   AR2f[:, SCR_A:SCR_A + 1024],
                                                   AR2f[:, SCR_B:SCR_B + 1024],
                                                   ALU.add)
                        g.op("V", fs, reads=[(A2F, SCR_A, SCR_A + 1024),
                                             (A2F, SCR_B, SCR_B + 1024)],
                             writes=[(A2F, SCR_A, SCR_A + 1024)])
                    def fz0(e, tlo=tlo, clo=clo):
                        return e.tensor_scalar(AR2f[:, SCR_B:SCR_B + 1024],
                                               AR2f[:, tlo:tlo + 1024],
                                               cols[:, clo + 6:clo + 7], 128.5,
                                               ALU.mult, ALU.add)
                    g.op("V", fz0, reads=[(A2F, tlo, tlo + 1024), stv(rg, 10)],
                         writes=[(A2F, SCR_B, SCR_B + 1024)])
                    def fr(e):
                        return e.tensor_tensor(AR2f[:, SCR_B:SCR_B + 1024],
                                               AR2f[:, SCR_B:SCR_B + 1024],
                                               AR2f[:, SCR_A:SCR_A + 1024],
                                               ALU.subtract)
                    g.op("V", fr, reads=[(A2F, SCR_A, SCR_A + 1024),
                                         (A2F, SCR_B, SCR_B + 1024)],
                         writes=[(A2F, SCR_B, SCR_B + 1024)])
                    def fxq(e, nlo=nlo, hf=hf, clo=clo):
                        return e.tensor_scalar(
                            AR2b[:, nlo + hf * 1024: nlo + (hf + 1) * 1024],
                            AR2f[:, SCR_B:SCR_B + 1024],
                            -128.0, cols[:, clo + 7:clo + 8],
                            ALU.add, ALU.mult)
                    g.op("V", fxq, reads=[(A2F, SCR_B, SCR_B + 1024), stv(rg, 11)],
                         writes=[(A2B, nlo + hf * 1024, nlo + (hf + 1) * 1024)])
                for c in range(DC):
                    pb = c % 2
                    def ftr2(e, nlo=nlo, c=c, pb=pb):
                        return e.transpose(PT[pb][:],
                                           AR2b[:, nlo + c * 128: nlo + (c + 1) * 128],
                                           ident[:])
                    g.op("T", ftr2, reads=[(A2B, nlo, nlo + 2048), cst("ident")],
                         writes=[bank(6 + pb)])
                    dst = c * T + tq * 128
                    def fev2(e, pb=pb, dst=dst):
                        return e.tensor_copy(XQ[:, dst:dst + 128], PT[pb][:])
                    g.op("V", fev2, reads=[bank(6 + pb)], writes=[(AXQ, dst, dst + 128)])
            emit_pw(2)
            if t % 4 == 3:
                emit_kproj_block(t // 4, [0, 1], {0: 0, 1: 1})

        emit_qkvo_quarter_load(wk, 2, 0)
        emit_qkvo_quarter_load(wk, 3, 1)
        for tb in range(6):
            emit_kproj_block(tb, [2, 3], {2: 0, 3: 1})
            emit_pw(6)

        def proj(kind, w_ap, skip_quarters):
            for a in range(4):
                buf = a % 2
                if (kind, a) not in skip_quarters:
                    emit_qkvo_quarter_load(w_ap, a, buf)
                if kind == "k":
                    for mi in range(4):
                        h = a * 4 + mi
                        for tb in range(6):
                            pbk = (h * 6 + tb) % 4
                            for k in range(DC - 1, -1, -1):
                                def fmm(e, buf=buf, mi=mi, k=k, tb=tb, pbk=pbk):
                                    lhs = AR1[:, wqtr_col(buf, k) + mi * 128:
                                              wqtr_col(buf, k) + (mi + 1) * 128]
                                    rhs = AR1[:, xnf_col(k) + tb * 512:
                                              xnf_col(k) + (tb + 1) * 512]
                                    return e.matmul(PS[pbk][:], lhs, rhs,
                                                    start=(k == DC - 1), stop=(k == 0))
                                g.op("T", fmm,
                                     reads=[(A1, wqtr_col(buf, k),
                                             wqtr_col(buf, k) + 512),
                                            (A1, xnf_col(k) + tb * 512,
                                             xnf_col(k) + (tb + 1) * 512)],
                                     writes=[bank(pbk)])
                            ev = (h * 6 + tb) % EVN
                            def fevk(e, pbk=pbk, ev=ev):
                                return e.tensor_copy(evac[:, ev * 512:(ev + 1) * 512],
                                                     PS[pbk][:])
                            g.op("V", fevk, reads=[bank(pbk)],
                                 writes=[("ev", ev, ev + 1)])
                            def fspk(e, h=h, tb=tb, ev=ev):
                                return e.dma_start(out=d_kf[h, tb],
                                                   in_=evac[:, ev * 512:(ev + 1) * 512])
                            g.op("Y", fspk, reads=[("ev", ev, ev + 1)],
                                 writes=[(DKF, h * 6 + tb, h * 6 + tb + 1)], dma=True)
                elif kind == "v":
                    for tt in range(CT):
                        pbk = tt % 4
                        for k in range(DC - 1, -1, -1):
                            def fmv(e, buf=buf, tt=tt, k=k, pbk=pbk):
                                lhs = AR1[:, xnf_col(k) + tt * 128:
                                          xnf_col(k) + (tt + 1) * 128]
                                rhs = AR1[:, wqtr_col(buf, k):wqtr_col(buf, k) + 512]
                                return e.matmul(PS[pbk][:], lhs, rhs,
                                                start=(k == DC - 1), stop=(k == 0))
                            g.op("T", fmv,
                                 reads=[(A1, wqtr_col(buf, k), wqtr_col(buf, k) + 512),
                                        (A1, xnf_col(k) + tt * 128,
                                         xnf_col(k) + (tt + 1) * 128)],
                                 writes=[bank(pbk)])
                        ev = tt % EVN
                        def fevv(e, pbk=pbk, ev=ev):
                            return e.tensor_copy(evac[:, ev * 512:(ev + 1) * 512],
                                                 PS[pbk][:])
                        g.op("V", fevv, reads=[bank(pbk)], writes=[("ev", ev, ev + 1)])
                        def fspv(e, tt=tt, a=a, ev=ev):
                            return e.dma_start(out=d_vtm[a, tt],
                                               in_=evac[:, ev * 512:(ev + 1) * 512])
                        g.op("Y", fspv, reads=[("ev", ev, ev + 1)],
                             writes=[(DVT, a * CT + tt, a * CT + tt + 1)], dma=True)
                else:
                    for mi in range(4):
                        h = a * 4 + mi
                        for qb in range(QB):
                            tb = 4 + qb
                            pbk = (h * 2 + qb) % 4
                            for k in range(DC - 1, -1, -1):
                                def fmq(e, buf=buf, mi=mi, k=k, tb=tb, pbk=pbk):
                                    lhs = AR1[:, wqtr_col(buf, k) + mi * 128:
                                              wqtr_col(buf, k) + (mi + 1) * 128]
                                    rhs = AR1[:, xnf_col(k) + tb * 512:
                                              xnf_col(k) + (tb + 1) * 512]
                                    return e.matmul(PS[pbk][:], lhs, rhs,
                                                    start=(k == DC - 1), stop=(k == 0))
                                g.op("T", fmq,
                                     reads=[(A1, wqtr_col(buf, k),
                                             wqtr_col(buf, k) + 512),
                                            (A1, xnf_col(k) + tb * 512,
                                             xnf_col(k) + (tb + 1) * 512)],
                                     writes=[bank(pbk)])
                            ev = (h * 2 + qb) % EVN
                            def fevq(e, pbk=pbk, ev=ev):
                                return e.tensor_copy(evac[:, ev * 512:(ev + 1) * 512],
                                                     PS[pbk][:])
                            g.op("V", fevq, reads=[bank(pbk)],
                                 writes=[("ev", ev, ev + 1)])
                            def fspq(e, h=h, qb=qb, ev=ev):
                                return e.dma_start(out=d_qf[h, qb],
                                                   in_=evac[:, ev * 512:(ev + 1) * 512])
                            g.op("Y", fspq, reads=[("ev", ev, ev + 1)],
                                 writes=[(DQF, h * 2 + qb, h * 2 + qb + 1)], dma=True)
                emit_pw(18)

        proj("v", wv, set())
        proj("q", wq, set())
        emit_pw(len(pw_jobs))

        for hc in range(FC):
            sg_ = hc % 2
            def flg(e, hc=hc, sg_=sg_):
                return e.dma_start(out=AR2b[:, wstage_col(sg_):wstage_col(sg_) + 2048],
                                   in_=d_wgs[hc])
            g.op("G", flg, reads=[(DWG, hc, hc + 1)],
                 writes=[(A2B, wstage_col(sg_), wstage_col(sg_) + 2048)], dma=True)
            def flu(e, hc=hc, sg_=sg_):
                return e.dma_start(
                    out=AR2b[:, wstage_col(2 + sg_):wstage_col(2 + sg_) + 2048],
                    in_=d_wus[hc])
            g.op("G", flu, reads=[(DWU, hc, hc + 1)],
                 writes=[(A2B, wstage_col(2 + sg_), wstage_col(2 + sg_) + 2048)],
                 dma=True)
            for qb in range(QB):
                bg = (hc * 4 + qb * 2) % 6
                bu = (hc * 4 + qb * 2 + 1) % 6
                for k in range(DC - 1, -1, -1):
                    def fmg(e, sg_=sg_, k=k, qb=qb, bg=bg):
                        lhs = AR2b[:, wstage_col(sg_) + k * 128:
                                   wstage_col(sg_) + (k + 1) * 128]
                        rhs = XQ[:, k * T + qb * 512: k * T + (qb + 1) * 512]
                        return e.matmul(PS[bg][:], lhs, rhs,
                                        start=(k == DC - 1), stop=(k == 0))
                    g.op("T", fmg,
                         reads=[(A2B, wstage_col(sg_) + k * 128,
                                 wstage_col(sg_) + (k + 1) * 128),
                                (AXQ, k * T + qb * 512, k * T + (qb + 1) * 512)],
                         writes=[bank(bg)])
                for k in range(DC - 1, -1, -1):
                    def fmu(e, sg_=sg_, k=k, qb=qb, bu=bu):
                        lhs = AR2b[:, wstage_col(2 + sg_) + k * 128:
                                   wstage_col(2 + sg_) + (k + 1) * 128]
                        rhs = XQ[:, k * T + qb * 512: k * T + (qb + 1) * 512]
                        return e.matmul(PS[bu][:], lhs, rhs,
                                        start=(k == DC - 1), stop=(k == 0))
                    g.op("T", fmu,
                         reads=[(A2B, wstage_col(2 + sg_) + k * 128,
                                 wstage_col(2 + sg_) + (k + 1) * 128),
                                (AXQ, k * T + qb * 512, k * T + (qb + 1) * 512)],
                         writes=[bank(bu)])
                sslot = P4_SG + ((hc * 2 + qb) % 4) * 512
                def fsg(e, bg=bg, sslot=sslot):
                    return e.activation(AR2f[:, sslot:sslot + 512], PS[bg][:], AF.Silu)
                g.op("S", fsg, reads=[bank(bg)], writes=[(A2F, sslot, sslot + 512)])
                dst = p_col(hc, qb * 512)
                def fgate(e, bu=bu, sslot=sslot, dst=dst):
                    return e.tensor_tensor(AR1[:, dst:dst + 512], PS[bu][:],
                                           AR2f[:, sslot:sslot + 512], ALU.mult)
                g.op("V", fgate, reads=[bank(bu), (A2F, sslot, sslot + 512)],
                     writes=[(A1, dst, dst + 512)])

        for mb in range(4):
            for qth in range(2):
                for hc8 in range(FC // 8):
                    ws = hc8 % 2
                    wlo = ws * 4096
                    def fld(e, hc8=hc8, mb=mb, wlo=wlo):
                        src = d_wds[hc8 * 8:(hc8 + 1) * 8, :, mb * 512:(mb + 1) * 512]
                        src = src.rearrange("c p n -> p c n")
                        dst = AR2b[:, wlo:wlo + 4096].rearrange(
                            "p (c n) -> p c n", c=8)
                        return e.dma_start(out=dst, in_=src)
                    g.op("S", fld, reads=[(DWD, hc8 * 8, hc8 * 8 + 8)],
                         writes=[(A2B, wlo, wlo + 4096)], dma=True)
                    for ci in range(7, -1, -1):
                        hc = hc8 * 8 + ci
                        for qi in range(4):
                            qt = qth * 4 + qi
                            def fmd(e, hc=hc, qt=qt, wlo=wlo, ci=ci, qi=qi):
                                lhs = AR1[:, p_col(hc, qt * 128):
                                          p_col(hc, qt * 128) + 128]
                                rhs = AR2b[:, wlo + ci * 512: wlo + (ci + 1) * 512]
                                return e.matmul(PS[qi][:], lhs, rhs,
                                                start=(hc == 7), stop=(hc == FC - 8))
                            g.op("T", fmd,
                                 reads=[(A1, p_col(hc, qt * 128),
                                         p_col(hc, qt * 128) + 128),
                                        (A2B, wlo + ci * 512, wlo + (ci + 1) * 512)],
                                 writes=[bank(qi)])
                for qi in range(4):
                    qt = qth * 4 + qi
                    sslot = P4_SG + qi * 512
                    def fsl(e, qi=qi, sslot=sslot):
                        return e.activation(AR2f[:, sslot:sslot + 512], PS[qi][:],
                                            AF.Silu)
                    g.op("S", fsl, reads=[bank(qi)], writes=[(A2F, sslot, sslot + 512)])
                    def fgs(e, qt=qt, mb=mb, sslot=sslot):
                        return e.dma_start(out=d_g[qt, :, mb * 512:(mb + 1) * 512],
                                           in_=AR2f[:, sslot:sslot + 512])
                    g.op("Y", fgs, reads=[(A2F, sslot, sslot + 512)],
                         writes=[(DG, qt * 4 + mb, qt * 4 + mb + 1)], dma=True)

        def fml(e):
            return e.dma_start(out=AR2b[:, MASKS_COL:MASKS_COL + 4096], in_=masks_in)
        g.op("S", fml, writes=[(A2B, MASKS_COL, MASKS_COL + 4096)], dma=True)

        MASK_SLOTS = {0: 0, 1: 1, 2: 2, 3: 3, 16: 4, 17: 5, 18: 6, 19: 7}

        def load_group(grp, gb):
            h0 = grp * 4
            for hl in range(4):
                dst = kbuf_col(gb, hl)
                def fkl(e, h0=h0, hl=hl, dst=dst):
                    src = d_kf[h0 + hl].rearrange("t p n -> p t n")
                    dsta = AR1[:, dst:dst + 3072].rearrange("p (t n) -> p t n", t=6)
                    return e.dma_start(out=dsta, in_=src)
                g.op("G", fkl, reads=[(DKF, (h0 + hl) * 6, (h0 + hl) * 6 + 6)],
                     writes=[(A1, dst, dst + 3072)], dma=True)
            for vq in range(4):
                dst = vbuf_col(gb, vq * 6)
                def fvl(e, grp=grp, vq=vq, dst=dst):
                    src = d_vtm[grp, vq * 6:(vq + 1) * 6].rearrange("t p n -> p t n")
                    dsta = AR1[:, dst:dst + 3072].rearrange("p (t n) -> p t n", t=6)
                    return e.dma_start(out=dsta, in_=src)
                g.op("G", fvl, reads=[(DVT, grp * CT + vq * 6,
                                       grp * CT + (vq + 1) * 6)],
                     writes=[(A1, dst, dst + 3072)], dma=True)

        def load_q(h, qb, slot):
            dst = qring_col(slot)
            def fql(e, h=h, qb=qb, dst=dst):
                return e.dma_start(out=AR2b[:, dst:dst + 512], in_=d_qf[h, qb])
            g.op("S", fql, reads=[(DQF, h * 2 + qb, h * 2 + qb + 1)],
                 writes=[(A2B, dst, dst + 512)], dma=True)

        load_group(0, 0)
        load_q(0, 0, 0)
        qtoggle = 0
        for grp in range(4):
            gb = grp % 2
            if grp < 3:
                load_group(grp + 1, (grp + 1) % 2)
            for hl in range(4):
                h = grp * 4 + hl
                for qb in range(QB):
                    qslot = qtoggle
                    nqh, nqb = (h, qb + 1) if qb + 1 < QB else (h + 1, 0)
                    if nqh < H:
                        load_q(nqh, nqb, qtoggle ^ 1)
                    qtoggle ^= 1
                    cbank = (h * 2 + qb) % 2 + 2
                    def emit_s(j):
                        sb = j % 2
                        ktloc = qb * 512 + 128 * j
                        kcol = kbuf_col(gb, hl) + ktloc
                        def fms(e, kcol=kcol, qslot=qslot, sb=sb):
                            return e.matmul(
                                PS[sb][:], AR1[:, kcol:kcol + 128],
                                AR2b[:, qring_col(qslot):qring_col(qslot) + 512],
                                start=True, stop=True)
                        g.op("T", fms, reads=[(A1, kcol, kcol + 128),
                                              (A2B, qring_col(qslot),
                                               qring_col(qslot) + 512)],
                             writes=[bank(sb)])
                        ptl = ptil_col(j % 4)
                        bcol = qb * J + j
                        def fex(e, sb=sb, ptl=ptl, bcol=bcol):
                            return e.activation(AR2b[:, ptl:ptl + 512], PS[sb][:],
                                                AF.Exp, bias=padb[:, bcol:bcol + 1],
                                                scale=SCALE)
                        g.op("S", fex, reads=[bank(sb), cst("padb")],
                             writes=[(A2B, ptl, ptl + 512)])
                        if j in MASK_SLOTS:
                            mcol = MASKS_COL + MASK_SLOTS[j] * 512
                            def fmk(e, ptl=ptl, mcol=mcol):
                                return e.tensor_tensor(AR2b[:, ptl:ptl + 512],
                                                       AR2b[:, ptl:ptl + 512],
                                                       AR2b[:, mcol:mcol + 512],
                                                       ALU.mult)
                            g.op("V", fmk, reads=[(A2B, ptl, ptl + 512),
                                                  (A2B, mcol, mcol + 512)],
                                 writes=[(A2B, ptl, ptl + 512)])
                    def emit_av(j):
                        ktloc = qb * 512 + 128 * j
                        ptl = ptil_col(j % 4)
                        ttl = ktloc // 128
                        vcol = vbuf_col(gb, ttl) + hl * 128
                        def fav(e, vcol=vcol, ptl=ptl, cbank=cbank, j=j):
                            return e.matmul(PS[cbank][:], AR1[:, vcol:vcol + 128],
                                            AR2b[:, ptl:ptl + 512],
                                            start=(j == 0), stop=(j == J - 1))
                        g.op("T", fav, reads=[(A1, vcol, vcol + 128),
                                              (A2B, ptl, ptl + 512)],
                             writes=[bank(cbank)])
                        def fdn(e, ptl=ptl, j=j):
                            return e.matmul(PS[4][0:1, :], onesc[:],
                                            AR2b[:, ptl:ptl + 512],
                                            start=(j == 0), stop=(j == J - 1))
                        g.op("T", fdn, reads=[cst("onesc"), (A2B, ptl, ptl + 512)],
                             writes=[bank(4)])
                    emit_s(0)
                    emit_s(1)
                    for j in range(J):
                        if j + 2 < J:
                            emit_s(j + 2)
                        emit_av(j)
                    g.op("V", lambda e: e.reciprocal(linv[:], PS[4][0:1, :]),
                         reads=[bank(4)], writes=[cst("linv")])
                    def fbc(e):
                        return e.matmul(PS[5][:], onesr[:], linv[:],
                                        start=True, stop=True)
                    g.op("T", fbc, reads=[cst("onesr"), cst("linv")],
                         writes=[bank(5)])
                    g.op("V", lambda e: e.tensor_copy(linvb[:], PS[5][:]),
                         reads=[bank(5)], writes=[cst("linvb")])
                    dst = ctx_col(h, qb * 512)
                    def fdv(e, cbank=cbank, dst=dst):
                        return e.tensor_tensor(AR1[:, dst:dst + 512], PS[cbank][:],
                                               linvb[:], ALU.mult)
                    g.op("V", fdv, reads=[bank(cbank), cst("linvb")],
                         writes=[(A1, dst, dst + 512)])

        for a in range(4):
            def fwo(e, a=a):
                src = wo[:, a * 512:(a + 1) * 512].rearrange("(k p) n -> p k n", p=128)
                base = 16384 + a * 512
                dst = AR1[:, base:base + 16 * 2048]
                dst = dst.rearrange("p (k n) -> p k n", k=16, n=2048)[:, :, 0:512]
                return e.dma_start(out=dst, in_=src)
            g.op("G", fwo,
                 writes=[(A1, 16384 + a * 512, 16384 + 15 * 2048 + (a + 1) * 512)],
                 dma=True)

        for qt in range(8):
            xlo = xslot_col(qt % 2)
            def fx3(e, qt=qt, xlo=xlo):
                return e.dma_start(out=AR2f[:, xlo:xlo + 2048],
                                  in_=x_tm[2048 + qt * 128: 2048 + (qt + 1) * 128, :])
            g.op("S", fx3, writes=[(A2F, xlo, xlo + 2048)], dma=True)
            glo = P3_G
            def fg3(e, qt=qt, glo=glo):
                return e.dma_start(out=AR2f[:, glo:glo + 2048], in_=d_g[qt])
            g.op("S", fg3, reads=[(DG, qt * 4, qt * 4 + 4)],
                 writes=[(A2F, glo, glo + 2048)], dma=True)
            for mb in range(4):
                bk = mb % 2
                for hh in range(H - 1, -1, -1):
                    def fwo2(e, hh=hh, qt=qt, mb=mb, bk=bk):
                        lhs = AR1[:, ctx_col(hh, qt * 128):ctx_col(hh, qt * 128) + 128]
                        rhs = AR1[:, wo_col(hh) + mb * 512:
                                  wo_col(hh) + (mb + 1) * 512]
                        return e.matmul(PS[bk][:], lhs, rhs,
                                        start=(hh == H - 1), stop=(hh == 0))
                    g.op("T", fwo2,
                         reads=[(A1, ctx_col(hh, qt * 128),
                                 ctx_col(hh, qt * 128) + 128),
                                (A1, wo_col(hh) + mb * 512,
                                 wo_col(hh) + (mb + 1) * 512)],
                         writes=[bank(bk)])
                def fc1(e, bk=bk, glo=glo, mb=mb):
                    return e.tensor_tensor(
                        AR2f[:, glo + mb * 512: glo + (mb + 1) * 512], PS[bk][:],
                        AR2f[:, glo + mb * 512: glo + (mb + 1) * 512], ALU.mult)
                g.op("V", fc1, reads=[bank(bk),
                                      (A2F, glo + mb * 512, glo + (mb + 1) * 512)],
                     writes=[(A2F, glo + mb * 512, glo + (mb + 1) * 512)])
                def fc2(e, mb=mb, glo=glo, xlo=xlo):
                    return e.tensor_tensor(
                        AR2f[:, xlo + mb * 512: xlo + (mb + 1) * 512],
                        AR2f[:, xlo + mb * 512: xlo + (mb + 1) * 512],
                        AR2f[:, glo + mb * 512: glo + (mb + 1) * 512], ALU.add)
                g.op("V", fc2, reads=[(A2F, xlo + mb * 512, xlo + (mb + 1) * 512),
                                      (A2F, glo + mb * 512, glo + (mb + 1) * 512)],
                     writes=[(A2F, xlo + mb * 512, xlo + (mb + 1) * 512)])
            def fst(e, qt=qt, xlo=xlo):
                return e.dma_start(out=out_hbm[qt * 128:(qt + 1) * 128, :],
                                   in_=AR2f[:, xlo:xlo + 2048])
            g.op("Y", fst, reads=[(A2F, xlo, xlo + 2048)],
                 writes=[("outhbm", qt, qt + 1)], dma=True)

        g.barrier_wait("Y", [(f"Y{i}", g.cnt[f"Y{i}"]) for i in range(NSLOT)
                             if g.cnt[f"Y{i}"] > 0])

        @block.tensor
        def _(e):
            for th in g.thunks["T"]:
                th(e, sems)
        @block.vector
        def _(e):
            for th in g.thunks["V"]:
                th(e, sems)
        @block.scalar
        def _(e):
            for th in g.thunks["S"]:
                th(e, sems)
        @block.sync
        def _(e):
            for th in g.thunks["Y"]:
                th(e, sems)
        @block.gpsimd
        def _(e):
            for th in g.thunks["G"]:
                th(e, sems)

    return nc


_NC_CACHE = None

def _get_nc():
    global _NC_CACHE
    if _NC_CACHE is None:
        _NC_CACHE = build_program()
    return _NC_CACHE


def _host_inputs(L0_attn_x, L0_attn_Wq, L0_attn_Wk, L0_attn_Wv, L0_attn_Wo,
                 L0_ffn_W_g, L0_ffn_W_u, L0_ffn_W_d):
    x = np.ascontiguousarray(np.asarray(L0_attn_x, dtype=np.float32))
    wq = np.ascontiguousarray(np.asarray(L0_attn_Wq, dtype=np.float32))
    wk = np.ascontiguousarray(np.asarray(L0_attn_Wk, dtype=np.float32))
    wv = np.ascontiguousarray(np.asarray(L0_attn_Wv, dtype=np.float32))
    wo = np.ascontiguousarray(np.asarray(L0_attn_Wo, dtype=np.float32))
    wgT = np.ascontiguousarray(np.asarray(L0_ffn_W_g, dtype=np.float32).T)
    wuT = np.ascontiguousarray(np.asarray(L0_ffn_W_u, dtype=np.float32).T)
    wdT = np.ascontiguousarray(np.asarray(L0_ffn_W_d, dtype=np.float32).T)

    p = np.arange(128)[:, None]
    jj = np.arange(512)[None, :]
    masks = np.zeros((128, 8 * 512), dtype=np.float32)
    for j in range(4):
        masks[:, j * 512:(j + 1) * 512] = (jj < p + 128 * j)
    for idx, j in enumerate(range(4)):
        masks[:, (4 + idx) * 512:(5 + idx) * 512] = (jj >= p + 128 * j)
    masks8 = masks.astype(BFNP)

    ident = np.eye(128, dtype=np.float32).astype(BFNP)
    onesc = np.ones((128, 1), dtype=np.float32).astype(BFNP)
    onesr = np.ones((1, 128), dtype=np.float32)

    in_maps = []
    for c in range(NCORES):
        b, i = divmod(c, 4)
        own0 = T * i
        lo = own0 - 2048
        x_tm_c = np.zeros((C, D), dtype=np.float32)
        src_lo = max(lo, 0)
        x_tm_c[src_lo - lo: C] = x[b, src_lo: own0 + T]
        padbias = np.zeros((128, 2 * J), dtype=np.float32)
        for qb in range(QB):
            for j in range(J):
                if own0 + 512 * qb - 2048 + 128 * j < 0:
                    padbias[:, qb * J + j] = PAD_BIAS
        in_maps.append({
            "x_tm": x_tm_c, "wq": wq, "wk": wk, "wv": wv, "wo": wo,
            "wgT": wgT, "wuT": wuT, "wdT": wdT,
            "masks8": masks8, "padbias": padbias,
            "ident": ident, "onesc": onesc, "onesr": onesr,
        })
    return in_maps


def kernel(L0_attn_x, L0_attn_Wq, L0_attn_Wk, L0_attn_Wv, L0_attn_Wo,
           L0_ffn_W_g, L0_ffn_W_u, L0_ffn_W_d):
    in_maps = _host_inputs(L0_attn_x, L0_attn_Wq, L0_attn_Wk, L0_attn_Wv,
                           L0_attn_Wo, L0_ffn_W_g, L0_ffn_W_u, L0_ffn_W_d)
    nc = _get_nc()
    res = run_bass_kernel_spmd(nc, in_maps, core_ids=list(range(NCORES)))
    full = np.empty((2, 4096, D), dtype=np.float32)
    for c in range(NCORES):
        b, i = divmod(c, 4)
        full[b, T * i: T * (i + 1)] = np.asarray(res.results[c]["out"])
    return full


# version 11
# speedup vs baseline: 1.7686x; 1.7686x over previous
import contextlib
import numpy as np
import ml_dtypes
import concourse.bass as bass
import concourse.mybir as mybir
from concourse.bass_utils import run_bass_kernel_spmd

BFNP = ml_dtypes.bfloat16
F32 = mybir.dt.float32
BF16 = mybir.dt.bfloat16
I32 = mybir.dt.int32
AF = mybir.ActivationFunctionType
ALU = mybir.AluOpType
AX = mybir.AxisListType

D = 2048
DC = 16
T = 1024
C = 3072
CT = 24
OT0 = 16
H = 16
F = 8192
FC = 64
J = 20
QB = 2
NCORES = 8
SCALE = 1.0 / float(np.sqrt(np.float32(128.0)))
LN_EPS = 1e-5
RMS_EPS = 1e-8
PAD_BIAS = -88.0

def xnf_col(c, t=0):   return c * 3072 + t * 128
def wqtr_col(buf, k=0): return 49152 + buf * 8192 + k * 512
def p_col(hc, q=0):    return hc * 1024 + q
def ctx_col(h, q=0):   return h * 1024 + q
def kbuf_col(gb, hl):  return 16384 + gb * 24576 + hl * 3072
def vbuf_col(gb, tt=0): return 16384 + gb * 24576 + 12288 + tt * 512
def wo_col(k):         return 16384 + k * 2048

def xslot_col(s):      return s * 2048
P4_SG = 0

def xnslot_col(s):     return s * 2048
def pwslot_col(s):     return 4096 + s * 2048
def wstage_col(s):     return s * 2048
def wdslice_col(s):    return s * 512
MASKS_COL = 0
def ptil_col(s):       return 4096 + s * 512
def qring_col(s):      return 6144 + s * 512

ENGS = ("T", "V", "S", "Y", "G")
NSLOT = 12
DMA_SEMS = [f"{q}{i}" for q in ("Y", "S", "G") for i in range(NSLOT)]
ALL_SEMS = ["T", "V", "S"] + DMA_SEMS


class Prog:

    def __init__(self, nc):
        self.nc = nc
        self.thunks = {e: [] for e in ENGS}
        self.cnt = {s: 0 for s in ALL_SEMS}
        self.qrr = {"Y": 0, "S": 0, "G": 0}
        self.water = {}
        self.res = {}

    def _deps_read(self, iv):
        arena, lo, hi = iv
        return [r["w"] for r in self.res.get(arena, [])
                if r["lo"] < hi and lo < r["hi"] and r["w"] is not None]

    def _deps_write(self, iv):
        arena, lo, hi = iv
        deps, keep = [], []
        for r in self.res.get(arena, []):
            if r["lo"] < hi and lo < r["hi"]:
                if r["w"] is not None:
                    deps.append(r["w"])
                deps.extend(r["r"])
                if r["lo"] < lo:
                    keep.append({"lo": r["lo"], "hi": lo, "w": r["w"], "r": list(r["r"])})
                if r["hi"] > hi:
                    keep.append({"lo": hi, "hi": r["hi"], "w": r["w"], "r": list(r["r"])})
            else:
                keep.append(r)
        self.res[arena] = keep
        return deps

    def _note_read(self, iv, tok):
        arena, lo, hi = iv
        for r in self.res.get(arena, []):
            if r["lo"] < hi and lo < r["hi"]:
                r["r"].append(tok)

    def op(self, eng, fn, reads=(), writes=(), dma=False):
        if dma:
            slot = self.qrr[eng]
            self.qrr[eng] = (slot + 1) % NSLOT
            sem = f"{eng}{slot}"
            step = 16
        else:
            sem = eng
            step = 1
        needs = {}
        for iv in reads:
            for (s, v) in self._deps_read(iv):
                if needs.get(s, 0) < v:
                    needs[s] = v
        for iv in writes:
            for (s, v) in self._deps_write(iv):
                if needs.get(s, 0) < v:
                    needs[s] = v
        if dma and self.cnt[sem] > 0 and needs.get(sem, 0) < self.cnt[sem]:
            needs[sem] = self.cnt[sem]
        waits = []
        for s, v in needs.items():
            if self.water.get((eng, s), 0) < v:
                self.water[(eng, s)] = v
                waits.append((s, v))
        self.cnt[sem] += step
        tok = (sem, self.cnt[sem])
        for iv in writes:
            arena, lo, hi = iv
            self.res.setdefault(arena, []).append(
                {"lo": lo, "hi": hi, "w": tok, "r": []})
        for iv in reads:
            self._note_read(iv, tok)

        def thunk(e, sems, waits=waits, fn=fn, sem=sem, step=step):
            for (s, v) in waits:
                e.wait_ge(sems[s], v)
            fn(e).then_inc(sems[sem], step)
        self.thunks[eng].append(thunk)

    def barrier_wait(self, eng, sem_targets):
        def thunk(e, sems, w=tuple(sem_targets)):
            for (s, v) in w:
                e.wait_ge(sems[s], v)
        self.thunks[eng].append(thunk)


def build_program():
    nc = bass.Bass("TRN2", target_bir_lowering=False)
    g = Prog(nc)

    x_tm = nc.dram_tensor("x_tm", [C, D], F32, kind="ExternalInput").ap()
    wq = nc.dram_tensor("wq", [D, D], F32, kind="ExternalInput").ap()
    wk = nc.dram_tensor("wk", [D, D], F32, kind="ExternalInput").ap()
    wv = nc.dram_tensor("wv", [D, D], F32, kind="ExternalInput").ap()
    wo = nc.dram_tensor("wo", [D, D], F32, kind="ExternalInput").ap()
    wgT = nc.dram_tensor("wgT", [D, F], F32, kind="ExternalInput").ap()
    wuT = nc.dram_tensor("wuT", [D, F], F32, kind="ExternalInput").ap()
    wdT = nc.dram_tensor("wdT", [F, D], F32, kind="ExternalInput").ap()
    masks_in = nc.dram_tensor("masks8", [128, 8 * 512], BF16, kind="ExternalInput").ap()
    padbias_in = nc.dram_tensor("padbias", [128, 2 * J], F32, kind="ExternalInput").ap()
    ident_in = nc.dram_tensor("ident", [128, 128], BF16, kind="ExternalInput").ap()
    onesc_in = nc.dram_tensor("onesc", [128, 1], BF16, kind="ExternalInput").ap()
    onesr_in = nc.dram_tensor("onesr", [1, 128], F32, kind="ExternalInput").ap()
    out_hbm = nc.dram_tensor("out", [T, D], F32, kind="ExternalOutput").ap()

    d_kf = nc.dram_tensor("d_kf", [H, 6, 128, 512], BF16, kind="Internal").ap()
    d_vtm = nc.dram_tensor("d_vtm", [4, CT, 128, 512], BF16, kind="Internal").ap()
    d_qf = nc.dram_tensor("d_qf", [H, QB, 128, 512], BF16, kind="Internal").ap()
    d_g = nc.dram_tensor("d_g", [8, 128, D], F32, kind="Internal").ap()
    d_wgs = nc.dram_tensor("d_wgs", [FC, 128, 2048], BF16, kind="Internal").ap()
    d_wus = nc.dram_tensor("d_wus", [FC, 128, 2048], BF16, kind="Internal").ap()
    d_wds = nc.dram_tensor("d_wds", [FC, 128, 2048], BF16, kind="Internal").ap()

    with contextlib.ExitStack() as stk:
        def S_(name, shape, dt):
            return stk.enter_context(nc.sbuf_tensor(name, shape, dt))
        AR1 = S_("AR1", [128, 65536], BF16)
        AR2f = S_("AR2f", [128, 4096], F32)
        AR2b = S_("AR2b", [128, 8192], BF16)
        XQ = S_("XQ", [128, DC * T], BF16)
        ident = S_("identt", [128, 128], BF16)
        onesc = S_("onesct", [128, 1], BF16)
        onesr = S_("onesrt", [1, 128], F32)
        padb = S_("padbt", [128, 2 * J], F32)
        bn6 = S_("bn6", [128, 2 * 24], F32)
        bn2 = S_("bn2", [128, 2 * 2], F32)
        cols = S_("cols", [128, 2 * 8], F32)
        linv = S_("linv", [1, 512], F32)
        linvb = S_("linvb", [128, 512], F32)
        EVN = 2
        evac = S_("evac", [128, EVN * 512], BF16)
        SCRI = S_("SCRI", [128, 2048], I32)

        PS = [stk.enter_context(nc.psum_tensor(f"ps{i}", [128, 512], F32))
              for i in range(6)]
        PT = [stk.enter_context(nc.psum_tensor(f"pt{i}", [128, 128], BF16))
              for i in range(2)]

        sems = {}
        sem_ctx = contextlib.ExitStack()
        for s in ALL_SEMS:
            sems[s] = sem_ctx.enter_context(nc.semaphore("sem" + s))
        stk.enter_context(sem_ctx)
        block = stk.enter_context(nc.Block())

        A1, A2F, A2B, AXQ = "AR1", "AR2f", "AR2b", "XQ"
        PSR = "psum"
        DKF, DVT, DQF, DG = "d_kf", "d_vtm", "d_qf", "d_g"
        DWG, DWU, DWD = "d_wgs", "d_wus", "d_wds"
        CST = "cst"
        CST_IDS = {"ident": 0, "onesc": 1, "onesr": 2, "padb": 3,
                   "linv": 4, "linvb": 5}
        def cst(n):
            i = CST_IDS[n]
            return (CST, i, i + 1)
        def bank(i):
            return (PSR, i, i + 1)
        def stv(ring, field):
            i = ring * 16 + field
            return ("st", i, i + 1)

        g.op("Y", lambda e: e.dma_start(out=ident[:], in_=ident_in),
             writes=[cst("ident")], dma=True)
        g.op("Y", lambda e: e.dma_start(out=onesc[:], in_=onesc_in),
             writes=[cst("onesc")], dma=True)
        g.op("Y", lambda e: e.dma_start(out=onesr[:], in_=onesr_in),
             writes=[cst("onesr")], dma=True)
        g.op("Y", lambda e: e.dma_start(out=padb[:], in_=padbias_in),
             writes=[cst("padb")], dma=True)

        def emit_pw_chunk(src_ap, dst_ap, dst_arena, idx, slot, colblock):
            lo = pwslot_col(slot)
            if colblock:
                for q4 in range(4):
                    def fn(e, src_ap=src_ap, idx=idx, lo=lo, q4=q4):
                        src = src_ap[q4 * 512:(q4 + 1) * 512,
                                     idx * 128:(idx + 1) * 128]
                        src = src.rearrange("(k p) h -> p k h", p=128)
                        dst = AR2b[:, lo + q4 * 512: lo + (q4 + 1) * 512]
                        dst = dst.rearrange("p (k h) -> p k h", k=4, h=128)
                        return e.dma_start(out=dst, in_=src)
                    g.op("G", fn, writes=[(A2B, lo + q4 * 512, lo + (q4 + 1) * 512)],
                         dma=True)
            else:
                def fn(e, src_ap=src_ap, idx=idx, lo=lo):
                    return e.dma_start(out=AR2b[:, lo:lo + 2048],
                                       in_=src_ap[idx * 128:(idx + 1) * 128, :])
                g.op("G", fn, writes=[(A2B, lo, lo + 2048)], dma=True)
            def fns(e, lo=lo):
                return e.sign(AR2b[:, lo:lo + 2048], AR2b[:, lo:lo + 2048])
            g.op("S", fns, reads=[(A2B, lo, lo + 2048)],
                 writes=[(A2B, lo, lo + 2048)])
            def fnw(e, dst_ap=dst_ap, idx=idx, lo=lo):
                return e.dma_start(out=dst_ap[idx], in_=AR2b[:, lo:lo + 2048])
            g.op("S", fnw, reads=[(A2B, lo, lo + 2048)],
                 writes=[(dst_arena, idx, idx + 1)], dma=True)

        pw_jobs = ([(wgT, d_wgs, DWG, i, True) for i in range(FC)]
                   + [(wuT, d_wus, DWU, i, True) for i in range(FC)]
                   + [(wdT, d_wds, DWD, i, False) for i in range(FC)])
        pw_state = {"pos": 0, "slot": 0}
        def emit_pw(n):
            for _ in range(n):
                if pw_state["pos"] >= len(pw_jobs):
                    return
                src_ap, dst_ap, arena, idx, colblock = pw_jobs[pw_state["pos"]]
                emit_pw_chunk(src_ap, dst_ap, arena, idx, pw_state["slot"], colblock)
                pw_state["slot"] ^= 1
                pw_state["pos"] += 1

        def emit_qkvo_quarter_load(w_ap, a, buf):
            def fn(e, w_ap=w_ap, a=a, buf=buf):
                src = w_ap[:, a * 512:(a + 1) * 512].rearrange(
                    "(k p) n -> p k n", p=128)
                dst = AR1[:, wqtr_col(buf, 0):wqtr_col(buf, 0) + 8192].rearrange(
                    "p (k n) -> p k n", k=DC)
                return e.dma_start(out=dst, in_=src)
            g.op("G", fn, writes=[(A1, wqtr_col(buf, 0), wqtr_col(buf, 0) + 8192)],
                 dma=True)

        emit_qkvo_quarter_load(wk, 0, 0)
        emit_qkvo_quarter_load(wk, 1, 1)


        ev_state = {"i": 0}
        def next_ev():
            ev = ev_state["i"] % EVN
            ev_state["i"] += 1
            return ev

        def emit_kproj_block(tb, quarters, bufmap):
            for a in quarters:
                buf = bufmap[a]
                for mp in range(2):
                    mis = (mp * 2, mp * 2 + 1)
                    for k in range(DC - 1, -1, -1):
                        for mi in mis:
                            pbk = mi
                            def fmm(e, buf=buf, mi=mi, k=k, tb=tb, pbk=pbk):
                                lhs = AR1[:, wqtr_col(buf, k) + mi * 128:
                                          wqtr_col(buf, k) + (mi + 1) * 128]
                                rhs = AR1[:, xnf_col(k) + tb * 512:
                                          xnf_col(k) + (tb + 1) * 512]
                                return e.matmul(PS[pbk][:], lhs, rhs,
                                                start=(k == DC - 1), stop=(k == 0))
                            g.op("T", fmm,
                                 reads=[(A1, wqtr_col(buf, k), wqtr_col(buf, k) + 512),
                                        (A1, xnf_col(k) + tb * 512,
                                         xnf_col(k) + (tb + 1) * 512)],
                                 writes=[bank(pbk)])
                    for mi in mis:
                        h = a * 4 + mi
                        pbk = mi
                        ev = next_ev()
                        def fevk(e, pbk=pbk, ev=ev):
                            return e.tensor_copy(evac[:, ev * 512:(ev + 1) * 512],
                                                 PS[pbk][:])
                        g.op("V", fevk, reads=[bank(pbk)], writes=[("ev", ev, ev + 1)])
                        def fspk(e, h=h, tb=tb, ev=ev):
                            return e.dma_start(out=d_kf[h, tb],
                                               in_=evac[:, ev * 512:(ev + 1) * 512])
                        g.op("Y", fspk, reads=[("ev", ev, ev + 1)],
                             writes=[(DKF, h * 6 + tb, h * 6 + tb + 1)], dma=True)

        for t in range(CT):
            xs = t % 2
            xlo = xslot_col(xs)
            rg = t % 2
            b6lo = rg * 24
            b2lo = rg * 2
            clo = rg * 8
            def fxl(e, t=t, xlo=xlo):
                return e.dma_start(out=AR2f[:, xlo:xlo + 2048],
                                  in_=x_tm[t * 128:(t + 1) * 128, :])
            g.op("Y", fxl, writes=[(A2F, xlo, xlo + 2048)], dma=True)
            for c4 in range(4):
                def fbs(e, xlo=xlo, b6lo=b6lo, c4=c4):
                    return e.bn_stats(bn6[:, b6lo + c4 * 6:b6lo + (c4 + 1) * 6],
                                      AR2f[:, xlo + c4 * 512: xlo + (c4 + 1) * 512])
                g.op("V", fbs, reads=[(A2F, xlo, xlo + 2048)],
                     writes=[stv(rg, c4)])
            def fag(e, b6lo=b6lo, b2lo=b2lo):
                return e.bn_aggr(bn2[:, b2lo:b2lo + 2], bn6[:, b6lo:b6lo + 24])
            g.op("V", fag, reads=[stv(rg, 0), stv(rg, 1), stv(rg, 2), stv(rg, 3)],
                 writes=[stv(rg, 4)])
            def fep(e, b2lo=b2lo, clo=clo):
                return e.tensor_scalar(cols[:, clo + 2:clo + 3],
                                       bn2[:, b2lo + 1:b2lo + 2],
                                       LN_EPS, None, ALU.add)
            g.op("V", fep, reads=[stv(rg, 4)], writes=[stv(rg, 5)])
            def fsq(e, clo=clo):
                return e.activation(cols[:, clo + 2:clo + 3],
                                    cols[:, clo + 2:clo + 3], AF.Sqrt)
            g.op("S", fsq, reads=[stv(rg, 5)], writes=[stv(rg, 5)])
            def frc(e, clo=clo):
                return e.reciprocal(cols[:, clo:clo + 1], cols[:, clo + 2:clo + 3])
            g.op("V", frc, reads=[stv(rg, 5)], writes=[stv(rg, 6)])
            def fbt(e, b2lo=b2lo, clo=clo):
                return e.tensor_scalar(cols[:, clo + 1:clo + 2],
                                       bn2[:, b2lo:b2lo + 1],
                                       cols[:, clo:clo + 1], -1.0,
                                       ALU.mult, ALU.mult)
            g.op("V", fbt, reads=[stv(rg, 4), stv(rg, 6)], writes=[stv(rg, 7)])
            ns = t % 2
            nlo = xnslot_col(ns)
            def fap(e, xlo=xlo, nlo=nlo, clo=clo):
                return e.activation(AR2b[:, nlo:nlo + 2048],
                                    AR2f[:, xlo:xlo + 2048], AF.Identity,
                                    bias=cols[:, clo + 1:clo + 2],
                                    scale=cols[:, clo:clo + 1])
            g.op("S", fap, reads=[(A2F, xlo, xlo + 2048), stv(rg, 6), stv(rg, 7)],
                 writes=[(A2B, nlo, nlo + 2048)])
            for c in range(DC):
                pb = c % 2
                def ftr(e, nlo=nlo, c=c, pb=pb):
                    return e.transpose(PT[pb][:],
                                       AR2b[:, nlo + c * 128: nlo + (c + 1) * 128],
                                       ident[:])
                g.op("T", ftr, reads=[(A2B, nlo, nlo + 2048), cst("ident")],
                     writes=[bank(6 + pb)])
                dst = xnf_col(c, t)
                def fev(e, pb=pb, dst=dst):
                    return e.copy(AR1[:, dst:dst + 128], PT[pb][:])
                g.op("S", fev, reads=[bank(6 + pb)], writes=[(A1, dst, dst + 128)])

            if t >= OT0:
                tq = t - OT0
                for c4 in range(4):
                    def fbs2(e, nlo=nlo, b6lo=b6lo, c4=c4):
                        return e.bn_stats(bn6[:, b6lo + c4 * 6:b6lo + (c4 + 1) * 6],
                                          AR2b[:, nlo + c4 * 512: nlo + (c4 + 1) * 512])
                    g.op("V", fbs2, reads=[(A2B, nlo, nlo + 2048)],
                         writes=[stv(rg, c4)])
                def fag2(e, b6lo=b6lo, b2lo=b2lo):
                    return e.bn_aggr(bn2[:, b2lo:b2lo + 2], bn6[:, b6lo:b6lo + 24])
                g.op("V", fag2, reads=[stv(rg, 0), stv(rg, 1), stv(rg, 2), stv(rg, 3)],
                     writes=[stv(rg, 4)])
                def fep2(e, b2lo=b2lo, clo=clo):
                    return e.tensor_scalar(cols[:, clo + 2:clo + 3],
                                           bn2[:, b2lo + 1:b2lo + 2],
                                           RMS_EPS, None, ALU.add)
                g.op("V", fep2, reads=[stv(rg, 4)], writes=[stv(rg, 5)])
                def fsq2(e, clo=clo):
                    return e.activation(cols[:, clo + 2:clo + 3],
                                        cols[:, clo + 2:clo + 3], AF.Sqrt)
                g.op("S", fsq2, reads=[stv(rg, 5)], writes=[stv(rg, 5)])
                def frc2(e, clo=clo):
                    return e.reciprocal(cols[:, clo + 4:clo + 5],
                                        cols[:, clo + 2:clo + 3])
                g.op("V", frc2, reads=[stv(rg, 5)], writes=[stv(rg, 8)])
                def ftt(e, nlo=nlo, xlo=xlo, b2lo=b2lo):
                    return e.tensor_scalar(AR2f[:, xlo:xlo + 2048],
                                           AR2b[:, nlo:nlo + 2048],
                                           bn2[:, b2lo:b2lo + 1], None,
                                           ALU.subtract)
                g.op("V", ftt, reads=[(A2B, nlo, nlo + 2048), stv(rg, 4)],
                     writes=[(A2F, xlo, xlo + 2048)])
                def fam(e, xlo=xlo, clo=clo):
                    return e.reduce_max(cols[:, clo + 5:clo + 6],
                                        AR2f[:, xlo:xlo + 2048],
                                        axis=AX.X, apply_absolute_value=True)
                g.op("V", fam, reads=[(A2F, xlo, xlo + 2048)], writes=[stv(rg, 9)])
                def fc1(e, clo=clo):
                    return e.reciprocal(cols[:, clo + 6:clo + 7],
                                        cols[:, clo + 5:clo + 6])
                g.op("V", fc1, reads=[stv(rg, 9)], writes=[stv(rg, 10)])
                def fc1b(e, clo=clo):
                    return e.tensor_scalar(cols[:, clo + 6:clo + 7],
                                           cols[:, clo + 6:clo + 7],
                                           127.0, None, ALU.mult)
                g.op("V", fc1b, reads=[stv(rg, 10)], writes=[stv(rg, 10)])
                def fiv(e, clo=clo):
                    return e.tensor_scalar(cols[:, clo + 7:clo + 8],
                                           cols[:, clo + 5:clo + 6],
                                           cols[:, clo + 4:clo + 5], 1.0 / 127.0,
                                           ALU.mult, ALU.mult)
                g.op("V", fiv, reads=[stv(rg, 9), stv(rg, 8)], writes=[stv(rg, 11)])
                def fqz(e, xlo=xlo, clo=clo):
                    return e.tensor_scalar(SCRI[:], AR2f[:, xlo:xlo + 2048],
                                           cols[:, clo + 6:clo + 7], None,
                                           ALU.mult)
                g.op("V", fqz, reads=[(A2F, xlo, xlo + 2048), stv(rg, 10)],
                     writes=[("scri", 0, 1)])
                def fqq(e, nlo=nlo, clo=clo):
                    return e.tensor_scalar(AR2b[:, nlo:nlo + 2048], SCRI[:],
                                           cols[:, clo + 7:clo + 8], None,
                                           ALU.mult)
                g.op("V", fqq, reads=[("scri", 0, 1), stv(rg, 11)],
                     writes=[(A2B, nlo, nlo + 2048)])
                for c in range(DC):
                    pb = c % 2
                    def ftr2(e, nlo=nlo, c=c, pb=pb):
                        return e.transpose(PT[pb][:],
                                           AR2b[:, nlo + c * 128: nlo + (c + 1) * 128],
                                           ident[:])
                    g.op("T", ftr2, reads=[(A2B, nlo, nlo + 2048), cst("ident")],
                         writes=[bank(6 + pb)])
                    dst = c * T + tq * 128
                    def fev2(e, pb=pb, dst=dst):
                        return e.tensor_copy(XQ[:, dst:dst + 128], PT[pb][:])
                    g.op("V", fev2, reads=[bank(6 + pb)], writes=[(AXQ, dst, dst + 128)])
            emit_pw(2)
            if t % 4 == 3:
                emit_kproj_block(t // 4, [0, 1], {0: 0, 1: 1})

        emit_qkvo_quarter_load(wk, 2, 0)
        emit_qkvo_quarter_load(wk, 3, 1)
        for tb in range(6):
            emit_kproj_block(tb, [2, 3], {2: 0, 3: 1})
            emit_pw(6)

        def proj(kind, w_ap, skip_quarters):
            for a in range(4):
                buf = a % 2
                if (kind, a) not in skip_quarters:
                    emit_qkvo_quarter_load(w_ap, a, buf)
                if kind == "k":
                    for mi in range(4):
                        h = a * 4 + mi
                        for tb in range(6):
                            pbk = (h * 6 + tb) % 4
                            for k in range(DC - 1, -1, -1):
                                def fmm(e, buf=buf, mi=mi, k=k, tb=tb, pbk=pbk):
                                    lhs = AR1[:, wqtr_col(buf, k) + mi * 128:
                                              wqtr_col(buf, k) + (mi + 1) * 128]
                                    rhs = AR1[:, xnf_col(k) + tb * 512:
                                              xnf_col(k) + (tb + 1) * 512]
                                    return e.matmul(PS[pbk][:], lhs, rhs,
                                                    start=(k == DC - 1), stop=(k == 0))
                                g.op("T", fmm,
                                     reads=[(A1, wqtr_col(buf, k),
                                             wqtr_col(buf, k) + 512),
                                            (A1, xnf_col(k) + tb * 512,
                                             xnf_col(k) + (tb + 1) * 512)],
                                     writes=[bank(pbk)])
                            ev = (h * 6 + tb) % EVN
                            def fevk(e, pbk=pbk, ev=ev):
                                return e.tensor_copy(evac[:, ev * 512:(ev + 1) * 512],
                                                     PS[pbk][:])
                            g.op("V", fevk, reads=[bank(pbk)],
                                 writes=[("ev", ev, ev + 1)])
                            def fspk(e, h=h, tb=tb, ev=ev):
                                return e.dma_start(out=d_kf[h, tb],
                                                   in_=evac[:, ev * 512:(ev + 1) * 512])
                            g.op("Y", fspk, reads=[("ev", ev, ev + 1)],
                                 writes=[(DKF, h * 6 + tb, h * 6 + tb + 1)], dma=True)
                elif kind == "v":
                    for tp_ in range(CT // 2):
                        tts = (2 * tp_, 2 * tp_ + 1)
                        for k in range(DC - 1, -1, -1):
                            for tt in tts:
                                pbk = tt % 4
                                def fmv(e, buf=buf, tt=tt, k=k, pbk=pbk):
                                    lhs = AR1[:, xnf_col(k) + tt * 128:
                                              xnf_col(k) + (tt + 1) * 128]
                                    rhs = AR1[:, wqtr_col(buf, k):wqtr_col(buf, k) + 512]
                                    return e.matmul(PS[pbk][:], lhs, rhs,
                                                    start=(k == DC - 1), stop=(k == 0))
                                g.op("T", fmv,
                                     reads=[(A1, wqtr_col(buf, k), wqtr_col(buf, k) + 512),
                                            (A1, xnf_col(k) + tt * 128,
                                             xnf_col(k) + (tt + 1) * 128)],
                                     writes=[bank(pbk)])
                        for tt in tts:
                            pbk = tt % 4
                            ev = next_ev()
                            def fevv(e, pbk=pbk, ev=ev):
                                return e.tensor_copy(evac[:, ev * 512:(ev + 1) * 512],
                                                     PS[pbk][:])
                            g.op("V", fevv, reads=[bank(pbk)], writes=[("ev", ev, ev + 1)])
                            def fspv(e, tt=tt, a=a, ev=ev):
                                return e.dma_start(out=d_vtm[a, tt],
                                                   in_=evac[:, ev * 512:(ev + 1) * 512])
                            g.op("Y", fspv, reads=[("ev", ev, ev + 1)],
                                 writes=[(DVT, a * CT + tt, a * CT + tt + 1)], dma=True)
                else:
                    for mi in range(4):
                        h = a * 4 + mi
                        for k in range(DC - 1, -1, -1):
                            for qb in range(QB):
                                tb = 4 + qb
                                pbk = (h * 2 + qb) % 4
                                def fmq(e, buf=buf, mi=mi, k=k, tb=tb, pbk=pbk):
                                    lhs = AR1[:, wqtr_col(buf, k) + mi * 128:
                                              wqtr_col(buf, k) + (mi + 1) * 128]
                                    rhs = AR1[:, xnf_col(k) + tb * 512:
                                              xnf_col(k) + (tb + 1) * 512]
                                    return e.matmul(PS[pbk][:], lhs, rhs,
                                                    start=(k == DC - 1), stop=(k == 0))
                                g.op("T", fmq,
                                     reads=[(A1, wqtr_col(buf, k),
                                             wqtr_col(buf, k) + 512),
                                            (A1, xnf_col(k) + tb * 512,
                                             xnf_col(k) + (tb + 1) * 512)],
                                     writes=[bank(pbk)])
                        for qb in range(QB):
                            pbk = (h * 2 + qb) % 4
                            ev = next_ev()
                            def fevq(e, pbk=pbk, ev=ev):
                                return e.tensor_copy(evac[:, ev * 512:(ev + 1) * 512],
                                                     PS[pbk][:])
                            g.op("V", fevq, reads=[bank(pbk)],
                                 writes=[("ev", ev, ev + 1)])
                            def fspq(e, h=h, qb=qb, ev=ev):
                                return e.dma_start(out=d_qf[h, qb],
                                                   in_=evac[:, ev * 512:(ev + 1) * 512])
                            g.op("Y", fspq, reads=[("ev", ev, ev + 1)],
                                 writes=[(DQF, h * 2 + qb, h * 2 + qb + 1)], dma=True)
                emit_pw(18)

        proj("v", wv, set())
        proj("q", wq, set())
        emit_pw(len(pw_jobs))

        for hc in range(FC):
            sg_ = hc % 2
            def flg(e, hc=hc, sg_=sg_):
                return e.dma_start(out=AR2b[:, wstage_col(sg_):wstage_col(sg_) + 2048],
                                   in_=d_wgs[hc])
            g.op("G", flg, reads=[(DWG, hc, hc + 1)],
                 writes=[(A2B, wstage_col(sg_), wstage_col(sg_) + 2048)], dma=True)
            def flu(e, hc=hc, sg_=sg_):
                return e.dma_start(
                    out=AR2b[:, wstage_col(2 + sg_):wstage_col(2 + sg_) + 2048],
                    in_=d_wus[hc])
            g.op("G", flu, reads=[(DWU, hc, hc + 1)],
                 writes=[(A2B, wstage_col(2 + sg_), wstage_col(2 + sg_) + 2048)],
                 dma=True)
            for qb in range(QB):
                bg = (hc * 4 + qb * 2) % 6
                bu = (hc * 4 + qb * 2 + 1) % 6
                for k in range(DC - 1, -1, -1):
                    def fmg(e, sg_=sg_, k=k, qb=qb, bg=bg):
                        lhs = AR2b[:, wstage_col(sg_) + k * 128:
                                   wstage_col(sg_) + (k + 1) * 128]
                        rhs = XQ[:, k * T + qb * 512: k * T + (qb + 1) * 512]
                        return e.matmul(PS[bg][:], lhs, rhs,
                                        start=(k == DC - 1), stop=(k == 0))
                    g.op("T", fmg,
                         reads=[(A2B, wstage_col(sg_) + k * 128,
                                 wstage_col(sg_) + (k + 1) * 128),
                                (AXQ, k * T + qb * 512, k * T + (qb + 1) * 512)],
                         writes=[bank(bg)])
                    def fmu(e, sg_=sg_, k=k, qb=qb, bu=bu):
                        lhs = AR2b[:, wstage_col(2 + sg_) + k * 128:
                                   wstage_col(2 + sg_) + (k + 1) * 128]
                        rhs = XQ[:, k * T + qb * 512: k * T + (qb + 1) * 512]
                        return e.matmul(PS[bu][:], lhs, rhs,
                                        start=(k == DC - 1), stop=(k == 0))
                    g.op("T", fmu,
                         reads=[(A2B, wstage_col(2 + sg_) + k * 128,
                                 wstage_col(2 + sg_) + (k + 1) * 128),
                                (AXQ, k * T + qb * 512, k * T + (qb + 1) * 512)],
                         writes=[bank(bu)])
                sslot = P4_SG + ((hc * 2 + qb) % 4) * 512
                def fsg(e, bg=bg, sslot=sslot):
                    return e.activation(AR2f[:, sslot:sslot + 512], PS[bg][:], AF.Silu)
                g.op("S", fsg, reads=[bank(bg)], writes=[(A2F, sslot, sslot + 512)])
                dst = p_col(hc, qb * 512)
                def fgate(e, bu=bu, sslot=sslot, dst=dst):
                    return e.tensor_tensor(AR1[:, dst:dst + 512], PS[bu][:],
                                           AR2f[:, sslot:sslot + 512], ALU.mult)
                g.op("V", fgate, reads=[bank(bu), (A2F, sslot, sslot + 512)],
                     writes=[(A1, dst, dst + 512)])

        for mb in range(4):
            for qth in range(2):
                for hc8 in range(FC // 8):
                    ws = hc8 % 2
                    wlo = ws * 4096
                    def fld(e, hc8=hc8, mb=mb, wlo=wlo):
                        src = d_wds[hc8 * 8:(hc8 + 1) * 8, :, mb * 512:(mb + 1) * 512]
                        src = src.rearrange("c p n -> p c n")
                        dst = AR2b[:, wlo:wlo + 4096].rearrange(
                            "p (c n) -> p c n", c=8)
                        return e.dma_start(out=dst, in_=src)
                    g.op("S", fld, reads=[(DWD, hc8 * 8, hc8 * 8 + 8)],
                         writes=[(A2B, wlo, wlo + 4096)], dma=True)
                    for ci in range(7, -1, -1):
                        hc = hc8 * 8 + ci
                        for qi in range(4):
                            qt = qth * 4 + qi
                            def fmd(e, hc=hc, qt=qt, wlo=wlo, ci=ci, qi=qi):
                                lhs = AR1[:, p_col(hc, qt * 128):
                                          p_col(hc, qt * 128) + 128]
                                rhs = AR2b[:, wlo + ci * 512: wlo + (ci + 1) * 512]
                                return e.matmul(PS[qi][:], lhs, rhs,
                                                start=(hc == 7), stop=(hc == FC - 8))
                            g.op("T", fmd,
                                 reads=[(A1, p_col(hc, qt * 128),
                                         p_col(hc, qt * 128) + 128),
                                        (A2B, wlo + ci * 512, wlo + (ci + 1) * 512)],
                                 writes=[bank(qi)])
                for qi in range(4):
                    qt = qth * 4 + qi
                    gdst = qt * 2048 + mb * 512
                    def fsl(e, qi=qi, gdst=gdst):
                        return e.activation(XQ[:, gdst:gdst + 512], PS[qi][:],
                                            AF.Silu)
                    g.op("S", fsl, reads=[bank(qi)],
                         writes=[(AXQ, gdst, gdst + 512)])

        def fml(e):
            return e.dma_start(out=AR2b[:, MASKS_COL:MASKS_COL + 4096], in_=masks_in)
        g.op("S", fml, writes=[(A2B, MASKS_COL, MASKS_COL + 4096)], dma=True)

        MASK_SLOTS = {0: 0, 1: 1, 2: 2, 3: 3, 16: 4, 17: 5, 18: 6, 19: 7}

        def load_group(grp, gb):
            h0 = grp * 4
            for hl in range(4):
                dst = kbuf_col(gb, hl)
                def fkl(e, h0=h0, hl=hl, dst=dst):
                    src = d_kf[h0 + hl].rearrange("t p n -> p t n")
                    dsta = AR1[:, dst:dst + 3072].rearrange("p (t n) -> p t n", t=6)
                    return e.dma_start(out=dsta, in_=src)
                g.op("G", fkl, reads=[(DKF, (h0 + hl) * 6, (h0 + hl) * 6 + 6)],
                     writes=[(A1, dst, dst + 3072)], dma=True)
            for vq in range(4):
                dst = vbuf_col(gb, vq * 6)
                def fvl(e, grp=grp, vq=vq, dst=dst):
                    src = d_vtm[grp, vq * 6:(vq + 1) * 6].rearrange("t p n -> p t n")
                    dsta = AR1[:, dst:dst + 3072].rearrange("p (t n) -> p t n", t=6)
                    return e.dma_start(out=dsta, in_=src)
                g.op("G", fvl, reads=[(DVT, grp * CT + vq * 6,
                                       grp * CT + (vq + 1) * 6)],
                     writes=[(A1, dst, dst + 3072)], dma=True)

        def load_q(h, qb, slot):
            dst = qring_col(slot)
            def fql(e, h=h, qb=qb, dst=dst):
                return e.dma_start(out=AR2b[:, dst:dst + 512], in_=d_qf[h, qb])
            g.op("S", fql, reads=[(DQF, h * 2 + qb, h * 2 + qb + 1)],
                 writes=[(A2B, dst, dst + 512)], dma=True)

        load_group(0, 0)
        load_q(0, 0, 0)
        qtoggle = 0
        for grp in range(4):
            gb = grp % 2
            if grp < 3:
                load_group(grp + 1, (grp + 1) % 2)
            for hl in range(4):
                h = grp * 4 + hl
                for qb in range(QB):
                    qslot = qtoggle
                    nqh, nqb = (h, qb + 1) if qb + 1 < QB else (h + 1, 0)
                    if nqh < H:
                        load_q(nqh, nqb, qtoggle ^ 1)
                    qtoggle ^= 1
                    cbank = (h * 2 + qb) % 2 + 2
                    def emit_s(j):
                        sb = j % 2
                        ktloc = qb * 512 + 128 * j
                        kcol = kbuf_col(gb, hl) + ktloc
                        def fms(e, kcol=kcol, qslot=qslot, sb=sb):
                            return e.matmul(
                                PS[sb][:], AR1[:, kcol:kcol + 128],
                                AR2b[:, qring_col(qslot):qring_col(qslot) + 512],
                                start=True, stop=True)
                        g.op("T", fms, reads=[(A1, kcol, kcol + 128),
                                              (A2B, qring_col(qslot),
                                               qring_col(qslot) + 512)],
                             writes=[bank(sb)])
                        ptl = ptil_col(j % 4)
                        bcol = qb * J + j
                        def fex(e, sb=sb, ptl=ptl, bcol=bcol):
                            return e.activation(AR2b[:, ptl:ptl + 512], PS[sb][:],
                                                AF.Exp, bias=padb[:, bcol:bcol + 1],
                                                scale=SCALE)
                        g.op("S", fex, reads=[bank(sb), cst("padb")],
                             writes=[(A2B, ptl, ptl + 512)])
                        if j in MASK_SLOTS:
                            mcol = MASKS_COL + MASK_SLOTS[j] * 512
                            def fmk(e, ptl=ptl, mcol=mcol):
                                return e.tensor_tensor(AR2b[:, ptl:ptl + 512],
                                                       AR2b[:, ptl:ptl + 512],
                                                       AR2b[:, mcol:mcol + 512],
                                                       ALU.mult)
                            g.op("V", fmk, reads=[(A2B, ptl, ptl + 512),
                                                  (A2B, mcol, mcol + 512)],
                                 writes=[(A2B, ptl, ptl + 512)])
                    def emit_av(j):
                        ktloc = qb * 512 + 128 * j
                        ptl = ptil_col(j % 4)
                        ttl = ktloc // 128
                        vcol = vbuf_col(gb, ttl) + hl * 128
                        def fav(e, vcol=vcol, ptl=ptl, cbank=cbank, j=j):
                            return e.matmul(PS[cbank][:], AR1[:, vcol:vcol + 128],
                                            AR2b[:, ptl:ptl + 512],
                                            start=(j == 0), stop=(j == J - 1))
                        g.op("T", fav, reads=[(A1, vcol, vcol + 128),
                                              (A2B, ptl, ptl + 512)],
                             writes=[bank(cbank)])
                        def fdn(e, ptl=ptl, j=j):
                            return e.matmul(PS[4][0:1, :], onesc[:],
                                            AR2b[:, ptl:ptl + 512],
                                            start=(j == 0), stop=(j == J - 1))
                        g.op("T", fdn, reads=[cst("onesc"), (A2B, ptl, ptl + 512)],
                             writes=[bank(4)])
                    emit_s(0)
                    emit_s(1)
                    for j in range(J):
                        if j + 2 < J:
                            emit_s(j + 2)
                        emit_av(j)
                    g.op("V", lambda e: e.reciprocal(linv[:], PS[4][0:1, :]),
                         reads=[bank(4)], writes=[cst("linv")])
                    def fbc(e):
                        return e.matmul(PS[5][:], onesr[:], linv[:],
                                        start=True, stop=True)
                    g.op("T", fbc, reads=[cst("onesr"), cst("linv")],
                         writes=[bank(5)])
                    g.op("V", lambda e: e.tensor_copy(linvb[:], PS[5][:]),
                         reads=[bank(5)], writes=[cst("linvb")])
                    dst = ctx_col(h, qb * 512)
                    def fdv(e, cbank=cbank, dst=dst):
                        return e.tensor_tensor(AR1[:, dst:dst + 512], PS[cbank][:],
                                               linvb[:], ALU.mult)
                    g.op("V", fdv, reads=[bank(cbank), cst("linvb")],
                         writes=[(A1, dst, dst + 512)])

        for a in range(4):
            def fwo(e, a=a):
                src = wo[:, a * 512:(a + 1) * 512].rearrange("(k p) n -> p k n", p=128)
                base = 16384 + a * 512
                dst = AR1[:, base:base + 16 * 2048]
                dst = dst.rearrange("p (k n) -> p k n", k=16, n=2048)[:, :, 0:512]
                return e.dma_start(out=dst, in_=src)
            g.op("G", fwo,
                 writes=[(A1, 16384 + a * 512, 16384 + 15 * 2048 + (a + 1) * 512)],
                 dma=True)

        for qt in range(8):
            xlo = xslot_col(qt % 2)
            def fx3(e, qt=qt, xlo=xlo):
                return e.dma_start(out=AR2f[:, xlo:xlo + 2048],
                                  in_=x_tm[2048 + qt * 128: 2048 + (qt + 1) * 128, :])
            g.op("S" if qt % 2 == 0 else "G", fx3,
                 writes=[(A2F, xlo, xlo + 2048)], dma=True)
            for mp in range(2):
                mbs = (2 * mp, 2 * mp + 1)
                for hh in range(H - 1, -1, -1):
                    for mb in mbs:
                        def fwo2(e, hh=hh, qt=qt, mb=mb, bk=mb):
                            lhs = AR1[:, ctx_col(hh, qt * 128):ctx_col(hh, qt * 128) + 128]
                            rhs = AR1[:, wo_col(hh) + mb * 512:
                                      wo_col(hh) + (mb + 1) * 512]
                            return e.matmul(PS[bk][:], lhs, rhs,
                                            start=(hh == H - 1), stop=(hh == 0))
                        g.op("T", fwo2,
                             reads=[(A1, ctx_col(hh, qt * 128),
                                     ctx_col(hh, qt * 128) + 128),
                                    (A1, wo_col(hh) + mb * 512,
                                     wo_col(hh) + (mb + 1) * 512)],
                             writes=[bank(mb)])
                for mb in mbs:
                    gsrc = qt * 2048 + mb * 512
                    tcol = 4096 + mb * 512
                    def fc1(e, bk=mb, gsrc=gsrc, tcol=tcol):
                        return e.tensor_tensor(
                            AR2b[:, tcol:tcol + 512], PS[bk][:],
                            XQ[:, gsrc:gsrc + 512], ALU.mult)
                    g.op("V", fc1, reads=[bank(mb), (AXQ, gsrc, gsrc + 512)],
                         writes=[(A2B, tcol, tcol + 512)])
                    def fc2(e, mb=mb, tcol=tcol, xlo=xlo):
                        return e.tensor_tensor(
                            AR2f[:, xlo + mb * 512: xlo + (mb + 1) * 512],
                            AR2f[:, xlo + mb * 512: xlo + (mb + 1) * 512],
                            AR2b[:, tcol:tcol + 512], ALU.add)
                    g.op("V", fc2, reads=[(A2F, xlo + mb * 512, xlo + (mb + 1) * 512),
                                          (A2B, tcol, tcol + 512)],
                         writes=[(A2F, xlo + mb * 512, xlo + (mb + 1) * 512)])
            def fst(e, qt=qt, xlo=xlo):
                return e.dma_start(out=out_hbm[qt * 128:(qt + 1) * 128, :],
                                   in_=AR2f[:, xlo:xlo + 2048])
            g.op("Y", fst, reads=[(A2F, xlo, xlo + 2048)],
                 writes=[("outhbm", qt, qt + 1)], dma=True)

        g.barrier_wait("Y", [(f"Y{i}", g.cnt[f"Y{i}"]) for i in range(NSLOT)
                             if g.cnt[f"Y{i}"] > 0])

        @block.tensor
        def _(e):
            for th in g.thunks["T"]:
                th(e, sems)
        @block.vector
        def _(e):
            for th in g.thunks["V"]:
                th(e, sems)
        @block.scalar
        def _(e):
            for th in g.thunks["S"]:
                th(e, sems)
        @block.sync
        def _(e):
            for th in g.thunks["Y"]:
                th(e, sems)
        @block.gpsimd
        def _(e):
            for th in g.thunks["G"]:
                th(e, sems)

    return nc


_NC_CACHE = None

def _get_nc():
    global _NC_CACHE
    if _NC_CACHE is None:
        _NC_CACHE = build_program()
    return _NC_CACHE


def _host_inputs(L0_attn_x, L0_attn_Wq, L0_attn_Wk, L0_attn_Wv, L0_attn_Wo,
                 L0_ffn_W_g, L0_ffn_W_u, L0_ffn_W_d):
    x = np.ascontiguousarray(np.asarray(L0_attn_x, dtype=np.float32))
    wq = np.ascontiguousarray(np.asarray(L0_attn_Wq, dtype=np.float32))
    wk = np.ascontiguousarray(np.asarray(L0_attn_Wk, dtype=np.float32))
    wv = np.ascontiguousarray(np.asarray(L0_attn_Wv, dtype=np.float32))
    wo = np.ascontiguousarray(np.asarray(L0_attn_Wo, dtype=np.float32))
    wgT = np.ascontiguousarray(np.asarray(L0_ffn_W_g, dtype=np.float32).T)
    wuT = np.ascontiguousarray(np.asarray(L0_ffn_W_u, dtype=np.float32).T)
    wdT = np.ascontiguousarray(np.asarray(L0_ffn_W_d, dtype=np.float32).T)

    p = np.arange(128)[:, None]
    jj = np.arange(512)[None, :]
    masks = np.zeros((128, 8 * 512), dtype=np.float32)
    for j in range(4):
        masks[:, j * 512:(j + 1) * 512] = (jj < p + 128 * j)
    for idx, j in enumerate(range(4)):
        masks[:, (4 + idx) * 512:(5 + idx) * 512] = (jj >= p + 128 * j)
    masks8 = masks.astype(BFNP)

    ident = np.eye(128, dtype=np.float32).astype(BFNP)
    onesc = np.ones((128, 1), dtype=np.float32).astype(BFNP)
    onesr = np.ones((1, 128), dtype=np.float32)

    in_maps = []
    for c in range(NCORES):
        b, i = divmod(c, 4)
        own0 = T * i
        lo = own0 - 2048
        x_tm_c = np.zeros((C, D), dtype=np.float32)
        src_lo = max(lo, 0)
        x_tm_c[src_lo - lo: C] = x[b, src_lo: own0 + T]
        padbias = np.zeros((128, 2 * J), dtype=np.float32)
        for qb in range(QB):
            for j in range(J):
                if own0 + 512 * qb - 2048 + 128 * j < 0:
                    padbias[:, qb * J + j] = PAD_BIAS
        in_maps.append({
            "x_tm": x_tm_c, "wq": wq, "wk": wk, "wv": wv, "wo": wo,
            "wgT": wgT, "wuT": wuT, "wdT": wdT,
            "masks8": masks8, "padbias": padbias,
            "ident": ident, "onesc": onesc, "onesr": onesr,
        })
    return in_maps


def kernel(L0_attn_x, L0_attn_Wq, L0_attn_Wk, L0_attn_Wv, L0_attn_Wo,
           L0_ffn_W_g, L0_ffn_W_u, L0_ffn_W_d):
    in_maps = _host_inputs(L0_attn_x, L0_attn_Wq, L0_attn_Wk, L0_attn_Wv,
                           L0_attn_Wo, L0_ffn_W_g, L0_ffn_W_u, L0_ffn_W_d)
    nc = _get_nc()
    res = run_bass_kernel_spmd(nc, in_maps, core_ids=list(range(NCORES)))
    full = np.empty((2, 4096, D), dtype=np.float32)
    for c in range(NCORES):
        b, i = divmod(c, 4)
        full[b, T * i: T * (i + 1)] = np.asarray(res.results[c]["out"])
    return full


# version 13
# speedup vs baseline: 1.7808x; 1.0069x over previous
import contextlib
import numpy as np
import ml_dtypes
import concourse.bass as bass
import concourse.mybir as mybir
from concourse.bass_utils import run_bass_kernel_spmd

BFNP = ml_dtypes.bfloat16
F32 = mybir.dt.float32
BF16 = mybir.dt.bfloat16
I32 = mybir.dt.int32
AF = mybir.ActivationFunctionType
ALU = mybir.AluOpType
AX = mybir.AxisListType

D = 2048
DC = 16
T = 1024
C = 3072
CT = 24
OT0 = 16
H = 16
F = 8192
FC = 64
J = 20
QB = 2
NCORES = 8
SCALE = 1.0 / float(np.sqrt(np.float32(128.0)))
LN_EPS = 1e-5
RMS_EPS = 1e-8
PAD_BIAS = -88.0

def xnf_col(c, t=0):   return c * 3072 + t * 128
def wqtr_col(buf, k=0): return 49152 + buf * 8192 + k * 512
def p_col(hc, q=0):    return hc * 1024 + q
def ctx_col(h, q=0):   return h * 1024 + q
def kbuf_col(gb, hl):  return 16384 + gb * 24576 + hl * 3072
def vbuf_col(gb, tt=0): return 16384 + gb * 24576 + 12288 + tt * 512
def wo_col(k):         return 16384 + k * 2048

def xslot_col(s):      return s * 2048
P4_SG = 0

def xnslot_col(s):     return s * 2048
def pwslot_col(s):     return 4096 + s * 2048
def wstage_col(s):     return s * 2048
def wdslice_col(s):    return s * 512
MASKS_COL = 0
def ptil_col(s):       return 4096 + s * 512
def qring_col(s):      return 6144 + s * 512
SDEN = 7168

ENGS = ("T", "V", "S", "Y", "G")
NSLOT = 12
DMA_SEMS = [f"{q}{i}" for q in ("Y", "S", "G") for i in range(NSLOT)]
ALL_SEMS = ["T", "V", "S"] + DMA_SEMS


class Prog:

    def __init__(self, nc):
        self.nc = nc
        self.thunks = {e: [] for e in ENGS}
        self.cnt = {s: 0 for s in ALL_SEMS}
        self.qrr = {"Y": 0, "S": 0, "G": 0}
        self.water = {}
        self.res = {}

    def _deps_read(self, iv):
        arena, lo, hi = iv
        return [r["w"] for r in self.res.get(arena, [])
                if r["lo"] < hi and lo < r["hi"] and r["w"] is not None]

    def _deps_write(self, iv):
        arena, lo, hi = iv
        deps, keep = [], []
        for r in self.res.get(arena, []):
            if r["lo"] < hi and lo < r["hi"]:
                if r["w"] is not None:
                    deps.append(r["w"])
                deps.extend(r["r"])
                if r["lo"] < lo:
                    keep.append({"lo": r["lo"], "hi": lo, "w": r["w"], "r": list(r["r"])})
                if r["hi"] > hi:
                    keep.append({"lo": hi, "hi": r["hi"], "w": r["w"], "r": list(r["r"])})
            else:
                keep.append(r)
        self.res[arena] = keep
        return deps

    def _note_read(self, iv, tok):
        arena, lo, hi = iv
        for r in self.res.get(arena, []):
            if r["lo"] < hi and lo < r["hi"]:
                r["r"].append(tok)

    def op(self, eng, fn, reads=(), writes=(), dma=False):
        if dma:
            slot = self.qrr[eng]
            self.qrr[eng] = (slot + 1) % NSLOT
            sem = f"{eng}{slot}"
            step = 16
        else:
            sem = eng
            step = 1
        needs = {}
        for iv in reads:
            for (s, v) in self._deps_read(iv):
                if needs.get(s, 0) < v:
                    needs[s] = v
        for iv in writes:
            for (s, v) in self._deps_write(iv):
                if needs.get(s, 0) < v:
                    needs[s] = v
        if dma and self.cnt[sem] > 0 and needs.get(sem, 0) < self.cnt[sem]:
            needs[sem] = self.cnt[sem]
        waits = []
        for s, v in needs.items():
            if self.water.get((eng, s), 0) < v:
                self.water[(eng, s)] = v
                waits.append((s, v))
        self.cnt[sem] += step
        tok = (sem, self.cnt[sem])
        for iv in writes:
            arena, lo, hi = iv
            self.res.setdefault(arena, []).append(
                {"lo": lo, "hi": hi, "w": tok, "r": []})
        for iv in reads:
            self._note_read(iv, tok)

        def thunk(e, sems, waits=waits, fn=fn, sem=sem, step=step):
            for (s, v) in waits:
                e.wait_ge(sems[s], v)
            fn(e).then_inc(sems[sem], step)
        self.thunks[eng].append(thunk)

    def barrier_wait(self, eng, sem_targets):
        def thunk(e, sems, w=tuple(sem_targets)):
            for (s, v) in w:
                e.wait_ge(sems[s], v)
        self.thunks[eng].append(thunk)


def build_program():
    nc = bass.Bass("TRN2", target_bir_lowering=False)
    g = Prog(nc)

    x_tm = nc.dram_tensor("x_tm", [C, D], F32, kind="ExternalInput").ap()
    wq = nc.dram_tensor("wq", [D, D], F32, kind="ExternalInput").ap()
    wk = nc.dram_tensor("wk", [D, D], F32, kind="ExternalInput").ap()
    wv = nc.dram_tensor("wv", [D, D], F32, kind="ExternalInput").ap()
    wo = nc.dram_tensor("wo", [D, D], F32, kind="ExternalInput").ap()
    wgT = nc.dram_tensor("wgT", [D, F], F32, kind="ExternalInput").ap()
    wuT = nc.dram_tensor("wuT", [D, F], F32, kind="ExternalInput").ap()
    wdT = nc.dram_tensor("wdT", [F, D], F32, kind="ExternalInput").ap()
    masks_in = nc.dram_tensor("masks8", [128, 8 * 512], BF16, kind="ExternalInput").ap()
    padbias_in = nc.dram_tensor("padbias", [128, 2 * J], F32, kind="ExternalInput").ap()
    ident_in = nc.dram_tensor("ident", [128, 128], BF16, kind="ExternalInput").ap()
    onesc_in = nc.dram_tensor("onesc", [128, 1], BF16, kind="ExternalInput").ap()
    onesr_in = nc.dram_tensor("onesr", [1, 128], F32, kind="ExternalInput").ap()
    out_hbm = nc.dram_tensor("out", [T, D], F32, kind="ExternalOutput").ap()

    d_kf = nc.dram_tensor("d_kf", [H, 6, 128, 512], BF16, kind="Internal").ap()
    d_vtm = nc.dram_tensor("d_vtm", [4, CT, 128, 512], BF16, kind="Internal").ap()
    d_qf = nc.dram_tensor("d_qf", [H, QB, 128, 512], BF16, kind="Internal").ap()
    d_g = nc.dram_tensor("d_g", [8, 128, D], F32, kind="Internal").ap()
    d_wgs = nc.dram_tensor("d_wgs", [FC, 128, 2048], BF16, kind="Internal").ap()
    d_wus = nc.dram_tensor("d_wus", [FC, 128, 2048], BF16, kind="Internal").ap()
    d_wds = nc.dram_tensor("d_wds", [FC, 128, 2048], BF16, kind="Internal").ap()

    with contextlib.ExitStack() as stk:
        def S_(name, shape, dt):
            return stk.enter_context(nc.sbuf_tensor(name, shape, dt))
        AR1 = S_("AR1", [128, 65536], BF16)
        AR2f = S_("AR2f", [128, 4096], F32)
        AR2b = S_("AR2b", [128, 8192], BF16)
        XQ = S_("XQ", [128, DC * T], BF16)
        ident = S_("identt", [128, 128], BF16)
        onesc = S_("onesct", [128, 1], BF16)
        onesr = S_("onesrt", [1, 128], F32)
        padb = S_("padbt", [128, 2 * J], F32)
        bn6 = S_("bn6", [128, 2 * 24], F32)
        bn2 = S_("bn2", [128, 2 * 2], F32)
        cols = S_("cols", [128, 2 * 8], F32)
        linv = S_("linv", [1, 512], F32)
        linvb = S_("linvb", [128, 512], F32)
        EVN = 2
        evac = S_("evac", [128, EVN * 512], BF16)
        SCRI = S_("SCRI", [128, 2048], I32)

        PS = [stk.enter_context(nc.psum_tensor(f"ps{i}", [128, 512], F32))
              for i in range(6)]
        PT = [stk.enter_context(nc.psum_tensor(f"pt{i}", [128, 128], BF16))
              for i in range(2)]

        sems = {}
        sem_ctx = contextlib.ExitStack()
        for s in ALL_SEMS:
            sems[s] = sem_ctx.enter_context(nc.semaphore("sem" + s))
        stk.enter_context(sem_ctx)
        block = stk.enter_context(nc.Block())

        A1, A2F, A2B, AXQ = "AR1", "AR2f", "AR2b", "XQ"
        PSR = "psum"
        DKF, DVT, DQF, DG = "d_kf", "d_vtm", "d_qf", "d_g"
        DWG, DWU, DWD = "d_wgs", "d_wus", "d_wds"
        CST = "cst"
        CST_IDS = {"ident": 0, "onesc": 1, "onesr": 2, "padb": 3,
                   "linv": 4, "linvb": 5}
        def cst(n):
            i = CST_IDS[n]
            return (CST, i, i + 1)
        def bank(i):
            return (PSR, i, i + 1)
        def stv(ring, field):
            i = ring * 16 + field
            return ("st", i, i + 1)

        g.op("Y", lambda e: e.dma_start(out=ident[:], in_=ident_in),
             writes=[cst("ident")], dma=True)
        g.op("Y", lambda e: e.dma_start(out=onesc[:], in_=onesc_in),
             writes=[cst("onesc")], dma=True)
        g.op("Y", lambda e: e.dma_start(out=onesr[:], in_=onesr_in),
             writes=[cst("onesr")], dma=True)
        g.op("Y", lambda e: e.dma_start(out=padb[:], in_=padbias_in),
             writes=[cst("padb")], dma=True)

        def emit_pw_chunk(src_ap, dst_ap, dst_arena, idx, slot, colblock):
            lo = pwslot_col(slot)
            if colblock:
                for q4 in range(4):
                    def fn(e, src_ap=src_ap, idx=idx, lo=lo, q4=q4):
                        src = src_ap[q4 * 512:(q4 + 1) * 512,
                                     idx * 128:(idx + 1) * 128]
                        src = src.rearrange("(k p) h -> p k h", p=128)
                        dst = AR2b[:, lo + q4 * 512: lo + (q4 + 1) * 512]
                        dst = dst.rearrange("p (k h) -> p k h", k=4, h=128)
                        return e.dma_start(out=dst, in_=src)
                    g.op("G", fn, writes=[(A2B, lo + q4 * 512, lo + (q4 + 1) * 512)],
                         dma=True)
            else:
                def fn(e, src_ap=src_ap, idx=idx, lo=lo):
                    return e.dma_start(out=AR2b[:, lo:lo + 2048],
                                       in_=src_ap[idx * 128:(idx + 1) * 128, :])
                g.op("G", fn, writes=[(A2B, lo, lo + 2048)], dma=True)
            def fns(e, lo=lo):
                return e.sign(AR2b[:, lo:lo + 2048], AR2b[:, lo:lo + 2048])
            g.op("S", fns, reads=[(A2B, lo, lo + 2048)],
                 writes=[(A2B, lo, lo + 2048)])
            def fnw(e, dst_ap=dst_ap, idx=idx, lo=lo):
                return e.dma_start(out=dst_ap[idx], in_=AR2b[:, lo:lo + 2048])
            g.op("S", fnw, reads=[(A2B, lo, lo + 2048)],
                 writes=[(dst_arena, idx, idx + 1)], dma=True)

        pw_jobs = ([(wgT, d_wgs, DWG, i, True) for i in range(FC)]
                   + [(wuT, d_wus, DWU, i, True) for i in range(FC)]
                   + [(wdT, d_wds, DWD, i, False) for i in range(FC)])
        pw_state = {"pos": 0, "slot": 0}
        def emit_pw(n):
            for _ in range(n):
                if pw_state["pos"] >= len(pw_jobs):
                    return
                src_ap, dst_ap, arena, idx, colblock = pw_jobs[pw_state["pos"]]
                emit_pw_chunk(src_ap, dst_ap, arena, idx, pw_state["slot"], colblock)
                pw_state["slot"] ^= 1
                pw_state["pos"] += 1

        def emit_qkvo_quarter_load(w_ap, a, buf):
            def fn(e, w_ap=w_ap, a=a, buf=buf):
                src = w_ap[:, a * 512:(a + 1) * 512].rearrange(
                    "(k p) n -> p k n", p=128)
                dst = AR1[:, wqtr_col(buf, 0):wqtr_col(buf, 0) + 8192].rearrange(
                    "p (k n) -> p k n", k=DC)
                return e.dma_start(out=dst, in_=src)
            g.op("G", fn, writes=[(A1, wqtr_col(buf, 0), wqtr_col(buf, 0) + 8192)],
                 dma=True)

        emit_qkvo_quarter_load(wk, 0, 0)
        emit_qkvo_quarter_load(wk, 1, 1)


        ev_state = {"i": 0}
        def next_ev():
            ev = ev_state["i"] % EVN
            ev_state["i"] += 1
            return ev

        def emit_kproj_block(tb, quarters, bufmap):
            for a in quarters:
                buf = bufmap[a]
                for mp in range(2):
                    mis = (mp * 2, mp * 2 + 1)
                    for k in range(DC - 1, -1, -1):
                        for mi in mis:
                            pbk = mi
                            def fmm(e, buf=buf, mi=mi, k=k, tb=tb, pbk=pbk):
                                lhs = AR1[:, wqtr_col(buf, k) + mi * 128:
                                          wqtr_col(buf, k) + (mi + 1) * 128]
                                rhs = AR1[:, xnf_col(k) + tb * 512:
                                          xnf_col(k) + (tb + 1) * 512]
                                return e.matmul(PS[pbk][:], lhs, rhs,
                                                start=(k == DC - 1), stop=(k == 0))
                            g.op("T", fmm,
                                 reads=[(A1, wqtr_col(buf, k), wqtr_col(buf, k) + 512),
                                        (A1, xnf_col(k) + tb * 512,
                                         xnf_col(k) + (tb + 1) * 512)],
                                 writes=[bank(pbk)])
                    for mi in mis:
                        h = a * 4 + mi
                        pbk = mi
                        ev = next_ev()
                        def fevk(e, pbk=pbk, ev=ev):
                            return e.tensor_copy(evac[:, ev * 512:(ev + 1) * 512],
                                                 PS[pbk][:])
                        g.op("V", fevk, reads=[bank(pbk)], writes=[("ev", ev, ev + 1)])
                        def fspk(e, h=h, tb=tb, ev=ev):
                            return e.dma_start(out=d_kf[h, tb],
                                               in_=evac[:, ev * 512:(ev + 1) * 512])
                        g.op("Y", fspk, reads=[("ev", ev, ev + 1)],
                             writes=[(DKF, h * 6 + tb, h * 6 + tb + 1)], dma=True)

        for t in range(CT):
            xs = t % 2
            xlo = xslot_col(xs)
            rg = t % 2
            b6lo = rg * 24
            b2lo = rg * 2
            clo = rg * 8
            def fxl(e, t=t, xlo=xlo):
                return e.dma_start(out=AR2f[:, xlo:xlo + 2048],
                                  in_=x_tm[t * 128:(t + 1) * 128, :])
            g.op("Y", fxl, writes=[(A2F, xlo, xlo + 2048)], dma=True)
            for c4 in range(4):
                def fbs(e, xlo=xlo, b6lo=b6lo, c4=c4):
                    return e.bn_stats(bn6[:, b6lo + c4 * 6:b6lo + (c4 + 1) * 6],
                                      AR2f[:, xlo + c4 * 512: xlo + (c4 + 1) * 512])
                g.op("V", fbs, reads=[(A2F, xlo, xlo + 2048)],
                     writes=[stv(rg, c4)])
            def fag(e, b6lo=b6lo, b2lo=b2lo):
                return e.bn_aggr(bn2[:, b2lo:b2lo + 2], bn6[:, b6lo:b6lo + 24])
            g.op("V", fag, reads=[stv(rg, 0), stv(rg, 1), stv(rg, 2), stv(rg, 3)],
                 writes=[stv(rg, 4)])
            def fep(e, b2lo=b2lo, clo=clo):
                return e.tensor_scalar(cols[:, clo + 2:clo + 3],
                                       bn2[:, b2lo + 1:b2lo + 2],
                                       LN_EPS, None, ALU.add)
            g.op("V", fep, reads=[stv(rg, 4)], writes=[stv(rg, 5)])
            def fsq(e, clo=clo):
                return e.activation(cols[:, clo + 2:clo + 3],
                                    cols[:, clo + 2:clo + 3], AF.Sqrt)
            g.op("S", fsq, reads=[stv(rg, 5)], writes=[stv(rg, 5)])
            def frc(e, clo=clo):
                return e.reciprocal(cols[:, clo:clo + 1], cols[:, clo + 2:clo + 3])
            g.op("V", frc, reads=[stv(rg, 5)], writes=[stv(rg, 6)])
            def fbt(e, b2lo=b2lo, clo=clo):
                return e.tensor_scalar(cols[:, clo + 1:clo + 2],
                                       bn2[:, b2lo:b2lo + 1],
                                       cols[:, clo:clo + 1], -1.0,
                                       ALU.mult, ALU.mult)
            g.op("V", fbt, reads=[stv(rg, 4), stv(rg, 6)], writes=[stv(rg, 7)])
            ns = t % 2
            nlo = xnslot_col(ns)
            def fap(e, xlo=xlo, nlo=nlo, clo=clo):
                return e.activation(AR2b[:, nlo:nlo + 2048],
                                    AR2f[:, xlo:xlo + 2048], AF.Identity,
                                    bias=cols[:, clo + 1:clo + 2],
                                    scale=cols[:, clo:clo + 1])
            g.op("S", fap, reads=[(A2F, xlo, xlo + 2048), stv(rg, 6), stv(rg, 7)],
                 writes=[(A2B, nlo, nlo + 2048)])
            for c in range(DC):
                pb = c % 2
                def ftr(e, nlo=nlo, c=c, pb=pb):
                    return e.transpose(PT[pb][:],
                                       AR2b[:, nlo + c * 128: nlo + (c + 1) * 128],
                                       ident[:])
                g.op("T", ftr, reads=[(A2B, nlo, nlo + 2048), cst("ident")],
                     writes=[bank(6 + pb)])
                dst = xnf_col(c, t)
                def fev(e, pb=pb, dst=dst):
                    return e.copy(AR1[:, dst:dst + 128], PT[pb][:])
                g.op("S", fev, reads=[bank(6 + pb)], writes=[(A1, dst, dst + 128)])

            if t >= OT0:
                tq = t - OT0
                for c4 in range(4):
                    def fbs2(e, nlo=nlo, b6lo=b6lo, c4=c4):
                        return e.bn_stats(bn6[:, b6lo + c4 * 6:b6lo + (c4 + 1) * 6],
                                          AR2b[:, nlo + c4 * 512: nlo + (c4 + 1) * 512])
                    g.op("V", fbs2, reads=[(A2B, nlo, nlo + 2048)],
                         writes=[stv(rg, c4)])
                def fag2(e, b6lo=b6lo, b2lo=b2lo):
                    return e.bn_aggr(bn2[:, b2lo:b2lo + 2], bn6[:, b6lo:b6lo + 24])
                g.op("V", fag2, reads=[stv(rg, 0), stv(rg, 1), stv(rg, 2), stv(rg, 3)],
                     writes=[stv(rg, 4)])
                def fep2(e, b2lo=b2lo, clo=clo):
                    return e.tensor_scalar(cols[:, clo + 2:clo + 3],
                                           bn2[:, b2lo + 1:b2lo + 2],
                                           RMS_EPS, None, ALU.add)
                g.op("V", fep2, reads=[stv(rg, 4)], writes=[stv(rg, 5)])
                def fsq2(e, clo=clo):
                    return e.activation(cols[:, clo + 2:clo + 3],
                                        cols[:, clo + 2:clo + 3], AF.Sqrt)
                g.op("S", fsq2, reads=[stv(rg, 5)], writes=[stv(rg, 5)])
                def frc2(e, clo=clo):
                    return e.reciprocal(cols[:, clo + 4:clo + 5],
                                        cols[:, clo + 2:clo + 3])
                g.op("V", frc2, reads=[stv(rg, 5)], writes=[stv(rg, 8)])
                def ftt(e, nlo=nlo, xlo=xlo, b2lo=b2lo):
                    return e.tensor_scalar(AR2f[:, xlo:xlo + 2048],
                                           AR2b[:, nlo:nlo + 2048],
                                           bn2[:, b2lo:b2lo + 1], None,
                                           ALU.subtract)
                g.op("V", ftt, reads=[(A2B, nlo, nlo + 2048), stv(rg, 4)],
                     writes=[(A2F, xlo, xlo + 2048)])
                def fam(e, xlo=xlo, clo=clo):
                    return e.reduce_max(cols[:, clo + 5:clo + 6],
                                        AR2f[:, xlo:xlo + 2048],
                                        axis=AX.X, apply_absolute_value=True)
                g.op("V", fam, reads=[(A2F, xlo, xlo + 2048)], writes=[stv(rg, 9)])
                def fc1(e, clo=clo):
                    return e.reciprocal(cols[:, clo + 6:clo + 7],
                                        cols[:, clo + 5:clo + 6])
                g.op("V", fc1, reads=[stv(rg, 9)], writes=[stv(rg, 10)])
                def fc1b(e, clo=clo):
                    return e.tensor_scalar(cols[:, clo + 6:clo + 7],
                                           cols[:, clo + 6:clo + 7],
                                           127.0, None, ALU.mult)
                g.op("V", fc1b, reads=[stv(rg, 10)], writes=[stv(rg, 10)])
                def fiv(e, clo=clo):
                    return e.tensor_scalar(cols[:, clo + 7:clo + 8],
                                           cols[:, clo + 5:clo + 6],
                                           cols[:, clo + 4:clo + 5], 1.0 / 127.0,
                                           ALU.mult, ALU.mult)
                g.op("V", fiv, reads=[stv(rg, 9), stv(rg, 8)], writes=[stv(rg, 11)])
                def fqz(e, xlo=xlo, clo=clo):
                    return e.tensor_scalar(SCRI[:], AR2f[:, xlo:xlo + 2048],
                                           cols[:, clo + 6:clo + 7], None,
                                           ALU.mult)
                g.op("V", fqz, reads=[(A2F, xlo, xlo + 2048), stv(rg, 10)],
                     writes=[("scri", 0, 1)])
                def fqq(e, nlo=nlo, clo=clo):
                    return e.tensor_scalar(AR2b[:, nlo:nlo + 2048], SCRI[:],
                                           cols[:, clo + 7:clo + 8], None,
                                           ALU.mult)
                g.op("V", fqq, reads=[("scri", 0, 1), stv(rg, 11)],
                     writes=[(A2B, nlo, nlo + 2048)])
                for c in range(DC):
                    pb = c % 2
                    def ftr2(e, nlo=nlo, c=c, pb=pb):
                        return e.transpose(PT[pb][:],
                                           AR2b[:, nlo + c * 128: nlo + (c + 1) * 128],
                                           ident[:])
                    g.op("T", ftr2, reads=[(A2B, nlo, nlo + 2048), cst("ident")],
                         writes=[bank(6 + pb)])
                    dst = c * T + tq * 128
                    def fev2(e, pb=pb, dst=dst):
                        return e.tensor_copy(XQ[:, dst:dst + 128], PT[pb][:])
                    g.op("V", fev2, reads=[bank(6 + pb)], writes=[(AXQ, dst, dst + 128)])
            emit_pw(2)
            if t % 4 == 3:
                emit_kproj_block(t // 4, [0, 1], {0: 0, 1: 1})

        emit_qkvo_quarter_load(wk, 2, 0)
        emit_qkvo_quarter_load(wk, 3, 1)
        for tb in range(6):
            emit_kproj_block(tb, [2, 3], {2: 0, 3: 1})
            emit_pw(6)

        def proj(kind, w_ap, skip_quarters):
            for a in range(4):
                buf = a % 2
                if (kind, a) not in skip_quarters:
                    emit_qkvo_quarter_load(w_ap, a, buf)
                if kind == "k":
                    for mi in range(4):
                        h = a * 4 + mi
                        for tb in range(6):
                            pbk = (h * 6 + tb) % 4
                            for k in range(DC - 1, -1, -1):
                                def fmm(e, buf=buf, mi=mi, k=k, tb=tb, pbk=pbk):
                                    lhs = AR1[:, wqtr_col(buf, k) + mi * 128:
                                              wqtr_col(buf, k) + (mi + 1) * 128]
                                    rhs = AR1[:, xnf_col(k) + tb * 512:
                                              xnf_col(k) + (tb + 1) * 512]
                                    return e.matmul(PS[pbk][:], lhs, rhs,
                                                    start=(k == DC - 1), stop=(k == 0))
                                g.op("T", fmm,
                                     reads=[(A1, wqtr_col(buf, k),
                                             wqtr_col(buf, k) + 512),
                                            (A1, xnf_col(k) + tb * 512,
                                             xnf_col(k) + (tb + 1) * 512)],
                                     writes=[bank(pbk)])
                            ev = (h * 6 + tb) % EVN
                            def fevk(e, pbk=pbk, ev=ev):
                                return e.tensor_copy(evac[:, ev * 512:(ev + 1) * 512],
                                                     PS[pbk][:])
                            g.op("V", fevk, reads=[bank(pbk)],
                                 writes=[("ev", ev, ev + 1)])
                            def fspk(e, h=h, tb=tb, ev=ev):
                                return e.dma_start(out=d_kf[h, tb],
                                                   in_=evac[:, ev * 512:(ev + 1) * 512])
                            g.op("Y", fspk, reads=[("ev", ev, ev + 1)],
                                 writes=[(DKF, h * 6 + tb, h * 6 + tb + 1)], dma=True)
                elif kind == "v":
                    for tp_ in range(CT // 2):
                        tts = (2 * tp_, 2 * tp_ + 1)
                        for k in range(DC - 1, -1, -1):
                            for tt in tts:
                                pbk = tt % 4
                                def fmv(e, buf=buf, tt=tt, k=k, pbk=pbk):
                                    lhs = AR1[:, xnf_col(k) + tt * 128:
                                              xnf_col(k) + (tt + 1) * 128]
                                    rhs = AR1[:, wqtr_col(buf, k):wqtr_col(buf, k) + 512]
                                    return e.matmul(PS[pbk][:], lhs, rhs,
                                                    start=(k == DC - 1), stop=(k == 0))
                                g.op("T", fmv,
                                     reads=[(A1, wqtr_col(buf, k), wqtr_col(buf, k) + 512),
                                            (A1, xnf_col(k) + tt * 128,
                                             xnf_col(k) + (tt + 1) * 128)],
                                     writes=[bank(pbk)])
                        for tt in tts:
                            pbk = tt % 4
                            ev = next_ev()
                            def fevv(e, pbk=pbk, ev=ev):
                                return e.tensor_copy(evac[:, ev * 512:(ev + 1) * 512],
                                                     PS[pbk][:])
                            g.op("V", fevv, reads=[bank(pbk)], writes=[("ev", ev, ev + 1)])
                            def fspv(e, tt=tt, a=a, ev=ev):
                                return e.dma_start(out=d_vtm[a, tt],
                                                   in_=evac[:, ev * 512:(ev + 1) * 512])
                            g.op("Y", fspv, reads=[("ev", ev, ev + 1)],
                                 writes=[(DVT, a * CT + tt, a * CT + tt + 1)], dma=True)
                else:
                    for mi in range(4):
                        h = a * 4 + mi
                        for k in range(DC - 1, -1, -1):
                            for qb in range(QB):
                                tb = 4 + qb
                                pbk = (h * 2 + qb) % 4
                                def fmq(e, buf=buf, mi=mi, k=k, tb=tb, pbk=pbk):
                                    lhs = AR1[:, wqtr_col(buf, k) + mi * 128:
                                              wqtr_col(buf, k) + (mi + 1) * 128]
                                    rhs = AR1[:, xnf_col(k) + tb * 512:
                                              xnf_col(k) + (tb + 1) * 512]
                                    return e.matmul(PS[pbk][:], lhs, rhs,
                                                    start=(k == DC - 1), stop=(k == 0))
                                g.op("T", fmq,
                                     reads=[(A1, wqtr_col(buf, k),
                                             wqtr_col(buf, k) + 512),
                                            (A1, xnf_col(k) + tb * 512,
                                             xnf_col(k) + (tb + 1) * 512)],
                                     writes=[bank(pbk)])
                        for qb in range(QB):
                            pbk = (h * 2 + qb) % 4
                            ev = next_ev()
                            def fevq(e, pbk=pbk, ev=ev):
                                return e.tensor_copy(evac[:, ev * 512:(ev + 1) * 512],
                                                     PS[pbk][:])
                            g.op("V", fevq, reads=[bank(pbk)],
                                 writes=[("ev", ev, ev + 1)])
                            def fspq(e, h=h, qb=qb, ev=ev):
                                return e.dma_start(out=d_qf[h, qb],
                                                   in_=evac[:, ev * 512:(ev + 1) * 512])
                            g.op("Y", fspq, reads=[("ev", ev, ev + 1)],
                                 writes=[(DQF, h * 2 + qb, h * 2 + qb + 1)], dma=True)
                emit_pw(18)

        proj("v", wv, set())
        proj("q", wq, set())
        emit_pw(len(pw_jobs))

        for hc in range(FC):
            sg_ = hc % 2
            def flg(e, hc=hc, sg_=sg_):
                return e.dma_start(out=AR2b[:, wstage_col(sg_):wstage_col(sg_) + 2048],
                                   in_=d_wgs[hc])
            g.op("G", flg, reads=[(DWG, hc, hc + 1)],
                 writes=[(A2B, wstage_col(sg_), wstage_col(sg_) + 2048)], dma=True)
            def flu(e, hc=hc, sg_=sg_):
                return e.dma_start(
                    out=AR2b[:, wstage_col(2 + sg_):wstage_col(2 + sg_) + 2048],
                    in_=d_wus[hc])
            g.op("G", flu, reads=[(DWU, hc, hc + 1)],
                 writes=[(A2B, wstage_col(2 + sg_), wstage_col(2 + sg_) + 2048)],
                 dma=True)
            for qb in range(QB):
                bg = (hc * 4 + qb * 2) % 6
                bu = (hc * 4 + qb * 2 + 1) % 6
                for k in range(DC - 1, -1, -1):
                    def fmg(e, sg_=sg_, k=k, qb=qb, bg=bg):
                        lhs = AR2b[:, wstage_col(sg_) + k * 128:
                                   wstage_col(sg_) + (k + 1) * 128]
                        rhs = XQ[:, k * T + qb * 512: k * T + (qb + 1) * 512]
                        return e.matmul(PS[bg][:], lhs, rhs,
                                        start=(k == DC - 1), stop=(k == 0))
                    g.op("T", fmg,
                         reads=[(A2B, wstage_col(sg_) + k * 128,
                                 wstage_col(sg_) + (k + 1) * 128),
                                (AXQ, k * T + qb * 512, k * T + (qb + 1) * 512)],
                         writes=[bank(bg)])
                    def fmu(e, sg_=sg_, k=k, qb=qb, bu=bu):
                        lhs = AR2b[:, wstage_col(2 + sg_) + k * 128:
                                   wstage_col(2 + sg_) + (k + 1) * 128]
                        rhs = XQ[:, k * T + qb * 512: k * T + (qb + 1) * 512]
                        return e.matmul(PS[bu][:], lhs, rhs,
                                        start=(k == DC - 1), stop=(k == 0))
                    g.op("T", fmu,
                         reads=[(A2B, wstage_col(2 + sg_) + k * 128,
                                 wstage_col(2 + sg_) + (k + 1) * 128),
                                (AXQ, k * T + qb * 512, k * T + (qb + 1) * 512)],
                         writes=[bank(bu)])
                sslot = P4_SG + ((hc * 2 + qb) % 4) * 512
                def fsg(e, bg=bg, sslot=sslot):
                    return e.activation(AR2f[:, sslot:sslot + 512], PS[bg][:], AF.Silu)
                g.op("S", fsg, reads=[bank(bg)], writes=[(A2F, sslot, sslot + 512)])
                dst = p_col(hc, qb * 512)
                def fgate(e, bu=bu, sslot=sslot, dst=dst):
                    return e.tensor_tensor(AR1[:, dst:dst + 512], PS[bu][:],
                                           AR2f[:, sslot:sslot + 512], ALU.mult)
                g.op("V", fgate, reads=[bank(bu), (A2F, sslot, sslot + 512)],
                     writes=[(A1, dst, dst + 512)])

        for mb in range(4):
            for qth in range(2):
                for hc8 in range(FC // 8):
                    ws = hc8 % 2
                    wlo = ws * 4096
                    def fld(e, hc8=hc8, mb=mb, wlo=wlo):
                        src = d_wds[hc8 * 8:(hc8 + 1) * 8, :, mb * 512:(mb + 1) * 512]
                        src = src.rearrange("c p n -> p c n")
                        dst = AR2b[:, wlo:wlo + 4096].rearrange(
                            "p (c n) -> p c n", c=8)
                        return e.dma_start(out=dst, in_=src)
                    g.op("S", fld, reads=[(DWD, hc8 * 8, hc8 * 8 + 8)],
                         writes=[(A2B, wlo, wlo + 4096)], dma=True)
                    for ci in range(7, -1, -1):
                        hc = hc8 * 8 + ci
                        for qi in range(4):
                            qt = qth * 4 + qi
                            def fmd(e, hc=hc, qt=qt, wlo=wlo, ci=ci, qi=qi):
                                lhs = AR1[:, p_col(hc, qt * 128):
                                          p_col(hc, qt * 128) + 128]
                                rhs = AR2b[:, wlo + ci * 512: wlo + (ci + 1) * 512]
                                return e.matmul(PS[qi][:], lhs, rhs,
                                                start=(hc == 7), stop=(hc == FC - 8))
                            g.op("T", fmd,
                                 reads=[(A1, p_col(hc, qt * 128),
                                         p_col(hc, qt * 128) + 128),
                                        (A2B, wlo + ci * 512, wlo + (ci + 1) * 512)],
                                 writes=[bank(qi)])
                for qi in range(4):
                    qt = qth * 4 + qi
                    gdst = qt * 2048 + mb * 512
                    def fsl(e, qi=qi, gdst=gdst):
                        return e.activation(XQ[:, gdst:gdst + 512], PS[qi][:],
                                            AF.Silu)
                    g.op("S", fsl, reads=[bank(qi)],
                         writes=[(AXQ, gdst, gdst + 512)])

        def fml(e):
            return e.dma_start(out=AR2b[:, MASKS_COL:MASKS_COL + 4096], in_=masks_in)
        g.op("S", fml, writes=[(A2B, MASKS_COL, MASKS_COL + 4096)], dma=True)

        MASK_SLOTS = {0: 0, 1: 1, 2: 2, 3: 3, 16: 4, 17: 5, 18: 6, 19: 7}

        def load_group(grp, gb):
            h0 = grp * 4
            for hl in range(4):
                dst = kbuf_col(gb, hl)
                def fkl(e, h0=h0, hl=hl, dst=dst):
                    src = d_kf[h0 + hl].rearrange("t p n -> p t n")
                    dsta = AR1[:, dst:dst + 3072].rearrange("p (t n) -> p t n", t=6)
                    return e.dma_start(out=dsta, in_=src)
                g.op("G", fkl, reads=[(DKF, (h0 + hl) * 6, (h0 + hl) * 6 + 6)],
                     writes=[(A1, dst, dst + 3072)], dma=True)
            for vq in range(4):
                dst = vbuf_col(gb, vq * 6)
                def fvl(e, grp=grp, vq=vq, dst=dst):
                    src = d_vtm[grp, vq * 6:(vq + 1) * 6].rearrange("t p n -> p t n")
                    dsta = AR1[:, dst:dst + 3072].rearrange("p (t n) -> p t n", t=6)
                    return e.dma_start(out=dsta, in_=src)
                g.op("G", fvl, reads=[(DVT, grp * CT + vq * 6,
                                       grp * CT + (vq + 1) * 6)],
                     writes=[(A1, dst, dst + 3072)], dma=True)

        def load_q(h, qb, slot):
            dst = qring_col(slot)
            def fql(e, h=h, qb=qb, dst=dst):
                return e.dma_start(out=AR2b[:, dst:dst + 512], in_=d_qf[h, qb])
            g.op("S", fql, reads=[(DQF, h * 2 + qb, h * 2 + qb + 1)],
                 writes=[(A2B, dst, dst + 512)], dma=True)

        load_group(0, 0)
        load_q(0, 0, 0)
        qtoggle = 0
        for grp in range(4):
            gb = grp % 2
            if grp < 3:
                load_group(grp + 1, (grp + 1) % 2)
            for hl in range(4):
                h = grp * 4 + hl
                for qb in range(QB):
                    qslot = qtoggle
                    nqh, nqb = (h, qb + 1) if qb + 1 < QB else (h + 1, 0)
                    if nqh < H:
                        load_q(nqh, nqb, qtoggle ^ 1)
                    qtoggle ^= 1
                    def emit_s(j):
                        sb = j % 2
                        ktloc = qb * 512 + 128 * j
                        kcol = kbuf_col(gb, hl) + ktloc
                        def fms(e, kcol=kcol, qslot=qslot, sb=sb):
                            return e.matmul(
                                PS[sb][:], AR1[:, kcol:kcol + 128],
                                AR2b[:, qring_col(qslot):qring_col(qslot) + 512],
                                start=True, stop=True)
                        g.op("T", fms, reads=[(A1, kcol, kcol + 128),
                                              (A2B, qring_col(qslot),
                                               qring_col(qslot) + 512)],
                             writes=[bank(sb)])
                        ptl = ptil_col(j % 4)
                        bcol = qb * J + j
                        def fex(e, sb=sb, ptl=ptl, bcol=bcol):
                            return e.activation(AR2b[:, ptl:ptl + 512], PS[sb][:],
                                                AF.Exp, bias=padb[:, bcol:bcol + 1],
                                                scale=SCALE)
                        g.op("S", fex, reads=[bank(sb), cst("padb")],
                             writes=[(A2B, ptl, ptl + 512)])
                        if j in MASK_SLOTS:
                            mcol = MASKS_COL + MASK_SLOTS[j] * 512
                            def fmk(e, ptl=ptl, mcol=mcol):
                                return e.tensor_tensor(AR2b[:, ptl:ptl + 512],
                                                       AR2b[:, ptl:ptl + 512],
                                                       AR2b[:, mcol:mcol + 512],
                                                       ALU.mult)
                            g.op("V", fmk, reads=[(A2B, ptl, ptl + 512),
                                                  (A2B, mcol, mcol + 512)],
                                 writes=[(A2B, ptl, ptl + 512)])
                        if j == 0:
                            def fda(e, ptl=ptl):
                                return e.tensor_copy(AR2b[:, SDEN:SDEN + 512],
                                                     AR2b[:, ptl:ptl + 512])
                        else:
                            def fda(e, ptl=ptl):
                                return e.tensor_tensor(AR2b[:, SDEN:SDEN + 512],
                                                       AR2b[:, SDEN:SDEN + 512],
                                                       AR2b[:, ptl:ptl + 512],
                                                       ALU.add)
                        g.op("V", fda, reads=[(A2B, ptl, ptl + 512),
                                              (A2B, SDEN, SDEN + 512)],
                             writes=[(A2B, SDEN, SDEN + 512)])
                    def emit_av(j):
                        ktloc = qb * 512 + 128 * j
                        ptl = ptil_col(j % 4)
                        ttl = ktloc // 128
                        vcol = vbuf_col(gb, ttl) + hl * 128
                        cbank = 2 + (j % 2)
                        def fav(e, vcol=vcol, ptl=ptl, cbank=cbank, j=j):
                            return e.matmul(PS[cbank][:], AR1[:, vcol:vcol + 128],
                                            AR2b[:, ptl:ptl + 512],
                                            start=(j < 2), stop=(j >= J - 2))
                        g.op("T", fav, reads=[(A1, vcol, vcol + 128),
                                              (A2B, ptl, ptl + 512)],
                             writes=[bank(cbank)])
                    emit_s(0)
                    emit_s(1)
                    for j in range(J):
                        if j + 2 < J:
                            emit_s(j + 2)
                        emit_av(j)
                    def fdn(e):
                        return e.matmul(PS[4][0:1, :], onesc[:],
                                        AR2b[:, SDEN:SDEN + 512],
                                        start=True, stop=True)
                    g.op("T", fdn, reads=[cst("onesc"), (A2B, SDEN, SDEN + 512)],
                         writes=[bank(4)])
                    g.op("V", lambda e: e.reciprocal(linv[:], PS[4][0:1, :]),
                         reads=[bank(4)], writes=[cst("linv")])
                    def fbc(e):
                        return e.matmul(PS[5][:], onesr[:], linv[:],
                                        start=True, stop=True)
                    g.op("T", fbc, reads=[cst("onesr"), cst("linv")],
                         writes=[bank(5)])
                    g.op("V", lambda e: e.tensor_copy(linvb[:], PS[5][:]),
                         reads=[bank(5)], writes=[cst("linvb")])
                    def fsum0(e):
                        return e.tensor_copy(AR2b[:, SDEN + 512:SDEN + 1024],
                                             PS[2][:])
                    g.op("V", fsum0, reads=[bank(2)],
                         writes=[(A2B, SDEN + 512, SDEN + 1024)])
                    def fsum(e):
                        return e.tensor_tensor(AR2b[:, SDEN + 512:SDEN + 1024],
                                               AR2b[:, SDEN + 512:SDEN + 1024],
                                               PS[3][:], ALU.add)
                    g.op("V", fsum, reads=[(A2B, SDEN + 512, SDEN + 1024), bank(3)],
                         writes=[(A2B, SDEN + 512, SDEN + 1024)])
                    dst = ctx_col(h, qb * 512)
                    def fdv(e, dst=dst):
                        return e.tensor_tensor(AR1[:, dst:dst + 512],
                                               AR2b[:, SDEN + 512:SDEN + 1024],
                                               linvb[:], ALU.mult)
                    g.op("V", fdv, reads=[(A2B, SDEN + 512, SDEN + 1024),
                                          cst("linvb")],
                         writes=[(A1, dst, dst + 512)])

        for a in range(4):
            def fwo(e, a=a):
                src = wo[:, a * 512:(a + 1) * 512].rearrange("(k p) n -> p k n", p=128)
                base = 16384 + a * 512
                dst = AR1[:, base:base + 16 * 2048]
                dst = dst.rearrange("p (k n) -> p k n", k=16, n=2048)[:, :, 0:512]
                return e.dma_start(out=dst, in_=src)
            g.op("G", fwo,
                 writes=[(A1, 16384 + a * 512, 16384 + 15 * 2048 + (a + 1) * 512)],
                 dma=True)

        for qt in range(8):
            xlo = xslot_col(qt % 2)
            def fx3(e, qt=qt, xlo=xlo):
                return e.dma_start(out=AR2f[:, xlo:xlo + 2048],
                                  in_=x_tm[2048 + qt * 128: 2048 + (qt + 1) * 128, :])
            g.op("S" if qt % 2 == 0 else "G", fx3,
                 writes=[(A2F, xlo, xlo + 2048)], dma=True)
            for mp in range(2):
                mbs = (2 * mp, 2 * mp + 1)
                for hh in range(H - 1, -1, -1):
                    for mb in mbs:
                        def fwo2(e, hh=hh, qt=qt, mb=mb, bk=mb):
                            lhs = AR1[:, ctx_col(hh, qt * 128):ctx_col(hh, qt * 128) + 128]
                            rhs = AR1[:, wo_col(hh) + mb * 512:
                                      wo_col(hh) + (mb + 1) * 512]
                            return e.matmul(PS[bk][:], lhs, rhs,
                                            start=(hh == H - 1), stop=(hh == 0))
                        g.op("T", fwo2,
                             reads=[(A1, ctx_col(hh, qt * 128),
                                     ctx_col(hh, qt * 128) + 128),
                                    (A1, wo_col(hh) + mb * 512,
                                     wo_col(hh) + (mb + 1) * 512)],
                             writes=[bank(mb)])
                for mb in mbs:
                    gsrc = qt * 2048 + mb * 512
                    tcol = 4096 + mb * 512
                    def fc1(e, bk=mb, gsrc=gsrc, tcol=tcol):
                        return e.tensor_tensor(
                            AR2b[:, tcol:tcol + 512], PS[bk][:],
                            XQ[:, gsrc:gsrc + 512], ALU.mult)
                    g.op("V", fc1, reads=[bank(mb), (AXQ, gsrc, gsrc + 512)],
                         writes=[(A2B, tcol, tcol + 512)])
                    def fc2(e, mb=mb, tcol=tcol, xlo=xlo):
                        return e.tensor_tensor(
                            AR2f[:, xlo + mb * 512: xlo + (mb + 1) * 512],
                            AR2f[:, xlo + mb * 512: xlo + (mb + 1) * 512],
                            AR2b[:, tcol:tcol + 512], ALU.add)
                    g.op("V", fc2, reads=[(A2F, xlo + mb * 512, xlo + (mb + 1) * 512),
                                          (A2B, tcol, tcol + 512)],
                         writes=[(A2F, xlo + mb * 512, xlo + (mb + 1) * 512)])
            def fst(e, qt=qt, xlo=xlo):
                return e.dma_start(out=out_hbm[qt * 128:(qt + 1) * 128, :],
                                   in_=AR2f[:, xlo:xlo + 2048])
            g.op("Y", fst, reads=[(A2F, xlo, xlo + 2048)],
                 writes=[("outhbm", qt, qt + 1)], dma=True)

        g.barrier_wait("Y", [(f"Y{i}", g.cnt[f"Y{i}"]) for i in range(NSLOT)
                             if g.cnt[f"Y{i}"] > 0])

        @block.tensor
        def _(e):
            for th in g.thunks["T"]:
                th(e, sems)
        @block.vector
        def _(e):
            for th in g.thunks["V"]:
                th(e, sems)
        @block.scalar
        def _(e):
            for th in g.thunks["S"]:
                th(e, sems)
        @block.sync
        def _(e):
            for th in g.thunks["Y"]:
                th(e, sems)
        @block.gpsimd
        def _(e):
            for th in g.thunks["G"]:
                th(e, sems)

    return nc


_NC_CACHE = None

def _get_nc():
    global _NC_CACHE
    if _NC_CACHE is None:
        _NC_CACHE = build_program()
    return _NC_CACHE


def _host_inputs(L0_attn_x, L0_attn_Wq, L0_attn_Wk, L0_attn_Wv, L0_attn_Wo,
                 L0_ffn_W_g, L0_ffn_W_u, L0_ffn_W_d):
    x = np.ascontiguousarray(np.asarray(L0_attn_x, dtype=np.float32))
    wq = np.ascontiguousarray(np.asarray(L0_attn_Wq, dtype=np.float32))
    wk = np.ascontiguousarray(np.asarray(L0_attn_Wk, dtype=np.float32))
    wv = np.ascontiguousarray(np.asarray(L0_attn_Wv, dtype=np.float32))
    wo = np.ascontiguousarray(np.asarray(L0_attn_Wo, dtype=np.float32))
    wgT = np.ascontiguousarray(np.asarray(L0_ffn_W_g, dtype=np.float32).T)
    wuT = np.ascontiguousarray(np.asarray(L0_ffn_W_u, dtype=np.float32).T)
    wdT = np.ascontiguousarray(np.asarray(L0_ffn_W_d, dtype=np.float32).T)

    p = np.arange(128)[:, None]
    jj = np.arange(512)[None, :]
    masks = np.zeros((128, 8 * 512), dtype=np.float32)
    for j in range(4):
        masks[:, j * 512:(j + 1) * 512] = (jj < p + 128 * j)
    for idx, j in enumerate(range(4)):
        masks[:, (4 + idx) * 512:(5 + idx) * 512] = (jj >= p + 128 * j)
    masks8 = masks.astype(BFNP)

    ident = np.eye(128, dtype=np.float32).astype(BFNP)
    onesc = np.ones((128, 1), dtype=np.float32).astype(BFNP)
    onesr = np.ones((1, 128), dtype=np.float32)

    in_maps = []
    for c in range(NCORES):
        b, i = divmod(c, 4)
        own0 = T * i
        lo = own0 - 2048
        x_tm_c = np.zeros((C, D), dtype=np.float32)
        src_lo = max(lo, 0)
        x_tm_c[src_lo - lo: C] = x[b, src_lo: own0 + T]
        padbias = np.zeros((128, 2 * J), dtype=np.float32)
        for qb in range(QB):
            for j in range(J):
                if own0 + 512 * qb - 2048 + 128 * j < 0:
                    padbias[:, qb * J + j] = PAD_BIAS
        in_maps.append({
            "x_tm": x_tm_c, "wq": wq, "wk": wk, "wv": wv, "wo": wo,
            "wgT": wgT, "wuT": wuT, "wdT": wdT,
            "masks8": masks8, "padbias": padbias,
            "ident": ident, "onesc": onesc, "onesr": onesr,
        })
    return in_maps


def kernel(L0_attn_x, L0_attn_Wq, L0_attn_Wk, L0_attn_Wv, L0_attn_Wo,
           L0_ffn_W_g, L0_ffn_W_u, L0_ffn_W_d):
    in_maps = _host_inputs(L0_attn_x, L0_attn_Wq, L0_attn_Wk, L0_attn_Wv,
                           L0_attn_Wo, L0_ffn_W_g, L0_ffn_W_u, L0_ffn_W_d)
    nc = _get_nc()
    res = run_bass_kernel_spmd(nc, in_maps, core_ids=list(range(NCORES)))
    full = np.empty((2, 4096, D), dtype=np.float32)
    for c in range(NCORES):
        b, i = divmod(c, 4)
        full[b, T * i: T * (i + 1)] = np.asarray(res.results[c]["out"])
    return full


# version 14 (final)
# speedup vs baseline: 1.8038x; 1.0129x over previous
import contextlib
import numpy as np
import ml_dtypes
import concourse.bass as bass
import concourse.mybir as mybir
from concourse.bass_utils import run_bass_kernel_spmd

BFNP = ml_dtypes.bfloat16
F32 = mybir.dt.float32
BF16 = mybir.dt.bfloat16
I32 = mybir.dt.int32
AF = mybir.ActivationFunctionType
ALU = mybir.AluOpType
AX = mybir.AxisListType

D = 2048
DC = 16
T = 1024
C = 3072
CT = 24
OT0 = 16
H = 16
F = 8192
FC = 64
J = 20
QB = 2
NCORES = 8
SCALE = 1.0 / float(np.sqrt(np.float32(128.0)))
LN_EPS = 1e-5
RMS_EPS = 1e-8
PAD_BIAS = -88.0

def xnf_col(c, t=0):   return c * 3072 + t * 128
def wqtr_col(buf, k=0): return 49152 + buf * 8192 + k * 512
def p_col(hc, q=0):    return hc * 1024 + q
def ctx_col(h, q=0):   return h * 1024 + q
def kbuf_col(gb, hl):  return 16384 + gb * 24576 + hl * 3072
def vbuf_col(gb, tt=0): return 16384 + gb * 24576 + 12288 + tt * 512
def wo_col(k):         return 16384 + k * 2048

def xslot_col(s):      return s * 2048
P4_SG = 0

def xnslot_col(s):     return s * 2048
def pwslot_col(s):     return 4096 + s * 2048
def wstage_col(s):     return s * 2048
def wdslice_col(s):    return s * 512
MASKS_COL = 0
def ptil_col(s):       return 4096 + s * 512
def qring_col(s):      return 6144 + s * 512
SDEN = 7168

ENGS = ("T", "V", "S", "Y", "G")
NSLOT = 12
DMA_SEMS = [f"{q}{i}" for q in ("Y", "S", "G") for i in range(NSLOT)]
ALL_SEMS = ["T", "V", "S"] + DMA_SEMS


class Prog:

    def __init__(self, nc):
        self.nc = nc
        self.thunks = {e: [] for e in ENGS}
        self.cnt = {s: 0 for s in ALL_SEMS}
        self.qrr = {"Y": 0, "S": 0, "G": 0}
        self.water = {}
        self.res = {}

    def _deps_read(self, iv):
        arena, lo, hi = iv
        return [r["w"] for r in self.res.get(arena, [])
                if r["lo"] < hi and lo < r["hi"] and r["w"] is not None]

    def _deps_write(self, iv):
        arena, lo, hi = iv
        deps, keep = [], []
        for r in self.res.get(arena, []):
            if r["lo"] < hi and lo < r["hi"]:
                if r["w"] is not None:
                    deps.append(r["w"])
                deps.extend(r["r"])
                if r["lo"] < lo:
                    keep.append({"lo": r["lo"], "hi": lo, "w": r["w"], "r": list(r["r"])})
                if r["hi"] > hi:
                    keep.append({"lo": hi, "hi": r["hi"], "w": r["w"], "r": list(r["r"])})
            else:
                keep.append(r)
        self.res[arena] = keep
        return deps

    def _note_read(self, iv, tok):
        arena, lo, hi = iv
        for r in self.res.get(arena, []):
            if r["lo"] < hi and lo < r["hi"]:
                r["r"].append(tok)

    def op(self, eng, fn, reads=(), writes=(), dma=False):
        if dma:
            slot = self.qrr[eng]
            self.qrr[eng] = (slot + 1) % NSLOT
            sem = f"{eng}{slot}"
            step = 16
        else:
            sem = eng
            step = 1
        needs = {}
        for iv in reads:
            for (s, v) in self._deps_read(iv):
                if needs.get(s, 0) < v:
                    needs[s] = v
        for iv in writes:
            for (s, v) in self._deps_write(iv):
                if needs.get(s, 0) < v:
                    needs[s] = v
        if dma and self.cnt[sem] > 0 and needs.get(sem, 0) < self.cnt[sem]:
            needs[sem] = self.cnt[sem]
        waits = []
        for s, v in needs.items():
            if self.water.get((eng, s), 0) < v:
                self.water[(eng, s)] = v
                waits.append((s, v))
        self.cnt[sem] += step
        tok = (sem, self.cnt[sem])
        for iv in writes:
            arena, lo, hi = iv
            self.res.setdefault(arena, []).append(
                {"lo": lo, "hi": hi, "w": tok, "r": []})
        for iv in reads:
            self._note_read(iv, tok)

        def thunk(e, sems, waits=waits, fn=fn, sem=sem, step=step):
            for (s, v) in waits:
                e.wait_ge(sems[s], v)
            fn(e).then_inc(sems[sem], step)
        self.thunks[eng].append(thunk)

    def barrier_wait(self, eng, sem_targets):
        def thunk(e, sems, w=tuple(sem_targets)):
            for (s, v) in w:
                e.wait_ge(sems[s], v)
        self.thunks[eng].append(thunk)


def build_program():
    nc = bass.Bass("TRN2", target_bir_lowering=False)
    g = Prog(nc)

    x_tm = nc.dram_tensor("x_tm", [C, D], F32, kind="ExternalInput").ap()
    wq = nc.dram_tensor("wq", [D, D], F32, kind="ExternalInput").ap()
    wk = nc.dram_tensor("wk", [D, D], F32, kind="ExternalInput").ap()
    wv = nc.dram_tensor("wv", [D, D], F32, kind="ExternalInput").ap()
    wo = nc.dram_tensor("wo", [D, D], F32, kind="ExternalInput").ap()
    wgT = nc.dram_tensor("wgT", [D, F], F32, kind="ExternalInput").ap()
    wuT = nc.dram_tensor("wuT", [D, F], F32, kind="ExternalInput").ap()
    wdT = nc.dram_tensor("wdT", [F, D], F32, kind="ExternalInput").ap()
    masks_in = nc.dram_tensor("masks8", [128, 8 * 512], BF16, kind="ExternalInput").ap()
    padbias_in = nc.dram_tensor("padbias", [128, 2 * J], F32, kind="ExternalInput").ap()
    ident_in = nc.dram_tensor("ident", [128, 128], BF16, kind="ExternalInput").ap()
    onesc_in = nc.dram_tensor("onesc", [128, 1], BF16, kind="ExternalInput").ap()
    onesr_in = nc.dram_tensor("onesr", [1, 128], F32, kind="ExternalInput").ap()
    out_hbm = nc.dram_tensor("out", [T, D], F32, kind="ExternalOutput").ap()

    d_kf = nc.dram_tensor("d_kf", [H, 6, 128, 512], BF16, kind="Internal").ap()
    d_vtm = nc.dram_tensor("d_vtm", [4, CT, 128, 512], BF16, kind="Internal").ap()
    d_qf = nc.dram_tensor("d_qf", [H, QB, 128, 512], BF16, kind="Internal").ap()
    d_g = nc.dram_tensor("d_g", [8, 128, D], F32, kind="Internal").ap()
    d_wgs = nc.dram_tensor("d_wgs", [FC, 128, 2048], BF16, kind="Internal").ap()
    d_wus = nc.dram_tensor("d_wus", [FC, 128, 2048], BF16, kind="Internal").ap()
    d_wds = nc.dram_tensor("d_wds", [FC, 128, 2048], BF16, kind="Internal").ap()

    with contextlib.ExitStack() as stk:
        def S_(name, shape, dt):
            return stk.enter_context(nc.sbuf_tensor(name, shape, dt))
        AR1 = S_("AR1", [128, 65536], BF16)
        AR2f = S_("AR2f", [128, 4096], F32)
        AR2b = S_("AR2b", [128, 8192], BF16)
        XQ = S_("XQ", [128, DC * T], BF16)
        ident = S_("identt", [128, 128], BF16)
        onesc = S_("onesct", [128, 1], BF16)
        onesr = S_("onesrt", [1, 128], F32)
        padb = S_("padbt", [128, 2 * J], F32)
        bn6 = S_("bn6", [128, 2 * 24], F32)
        bn2 = S_("bn2", [128, 2 * 2], F32)
        cols = S_("cols", [128, 2 * 8], F32)
        linv = S_("linv", [1, 512], F32)
        linvb = S_("linvb", [128, 512], F32)
        EVN = 2
        evac = S_("evac", [128, EVN * 512], BF16)
        SCRI = S_("SCRI", [128, 2048], I32)

        PS = [stk.enter_context(nc.psum_tensor(f"ps{i}", [128, 512], F32))
              for i in range(6)]
        PT = [stk.enter_context(nc.psum_tensor(f"pt{i}", [128, 128], BF16))
              for i in range(2)]

        sems = {}
        sem_ctx = contextlib.ExitStack()
        for s in ALL_SEMS:
            sems[s] = sem_ctx.enter_context(nc.semaphore("sem" + s))
        stk.enter_context(sem_ctx)
        block = stk.enter_context(nc.Block())

        A1, A2F, A2B, AXQ = "AR1", "AR2f", "AR2b", "XQ"
        PSR = "psum"
        DKF, DVT, DQF, DG = "d_kf", "d_vtm", "d_qf", "d_g"
        DWG, DWU, DWD = "d_wgs", "d_wus", "d_wds"
        CST = "cst"
        CST_IDS = {"ident": 0, "onesc": 1, "onesr": 2, "padb": 3,
                   "linv": 4, "linvb": 5}
        def cst(n):
            i = CST_IDS[n]
            return (CST, i, i + 1)
        def bank(i):
            return (PSR, i, i + 1)
        def stv(ring, field):
            i = ring * 16 + field
            return ("st", i, i + 1)

        g.op("Y", lambda e: e.dma_start(out=ident[:], in_=ident_in),
             writes=[cst("ident")], dma=True)
        g.op("Y", lambda e: e.dma_start(out=onesc[:], in_=onesc_in),
             writes=[cst("onesc")], dma=True)
        g.op("Y", lambda e: e.dma_start(out=onesr[:], in_=onesr_in),
             writes=[cst("onesr")], dma=True)
        g.op("Y", lambda e: e.dma_start(out=padb[:], in_=padbias_in),
             writes=[cst("padb")], dma=True)

        def emit_pw_chunk(src_ap, dst_ap, dst_arena, idx, slot, colblock):
            lo = pwslot_col(slot)
            if colblock:
                for q4 in range(4):
                    def fn(e, src_ap=src_ap, idx=idx, lo=lo, q4=q4):
                        src = src_ap[q4 * 512:(q4 + 1) * 512,
                                     idx * 128:(idx + 1) * 128]
                        src = src.rearrange("(k p) h -> p k h", p=128)
                        dst = AR2b[:, lo + q4 * 512: lo + (q4 + 1) * 512]
                        dst = dst.rearrange("p (k h) -> p k h", k=4, h=128)
                        return e.dma_start(out=dst, in_=src)
                    g.op("G", fn, writes=[(A2B, lo + q4 * 512, lo + (q4 + 1) * 512)],
                         dma=True)
            else:
                def fn(e, src_ap=src_ap, idx=idx, lo=lo):
                    return e.dma_start(out=AR2b[:, lo:lo + 2048],
                                       in_=src_ap[idx * 128:(idx + 1) * 128, :])
                g.op("G", fn, writes=[(A2B, lo, lo + 2048)], dma=True)
            def fns(e, lo=lo):
                return e.sign(AR2b[:, lo:lo + 2048], AR2b[:, lo:lo + 2048])
            g.op("S", fns, reads=[(A2B, lo, lo + 2048)],
                 writes=[(A2B, lo, lo + 2048)])
            def fnw(e, dst_ap=dst_ap, idx=idx, lo=lo):
                return e.dma_start(out=dst_ap[idx], in_=AR2b[:, lo:lo + 2048])
            g.op("S", fnw, reads=[(A2B, lo, lo + 2048)],
                 writes=[(dst_arena, idx, idx + 1)], dma=True)

        pw_jobs = ([(wgT, d_wgs, DWG, i, True) for i in range(FC)]
                   + [(wuT, d_wus, DWU, i, True) for i in range(FC)]
                   + [(wdT, d_wds, DWD, i, False) for i in range(FC)])
        pw_state = {"pos": 0, "slot": 0}
        def emit_pw(n):
            for _ in range(n):
                if pw_state["pos"] >= len(pw_jobs):
                    return
                src_ap, dst_ap, arena, idx, colblock = pw_jobs[pw_state["pos"]]
                emit_pw_chunk(src_ap, dst_ap, arena, idx, pw_state["slot"], colblock)
                pw_state["slot"] ^= 1
                pw_state["pos"] += 1

        def emit_qkvo_quarter_load(w_ap, a, buf):
            def fn(e, w_ap=w_ap, a=a, buf=buf):
                src = w_ap[:, a * 512:(a + 1) * 512].rearrange(
                    "(k p) n -> p k n", p=128)
                dst = AR1[:, wqtr_col(buf, 0):wqtr_col(buf, 0) + 8192].rearrange(
                    "p (k n) -> p k n", k=DC)
                return e.dma_start(out=dst, in_=src)
            g.op("G", fn, writes=[(A1, wqtr_col(buf, 0), wqtr_col(buf, 0) + 8192)],
                 dma=True)

        emit_qkvo_quarter_load(wk, 0, 0)
        emit_qkvo_quarter_load(wk, 1, 1)


        ev_state = {"i": 0}
        def next_ev():
            ev = ev_state["i"] % EVN
            ev_state["i"] += 1
            return ev

        def emit_kproj_block(tb, quarters, bufmap):
            for a in quarters:
                buf = bufmap[a]
                for mp in range(2):
                    mis = (mp * 2, mp * 2 + 1)
                    for k in range(DC - 1, -1, -1):
                        for mi in mis:
                            pbk = mi
                            def fmm(e, buf=buf, mi=mi, k=k, tb=tb, pbk=pbk):
                                lhs = AR1[:, wqtr_col(buf, k) + mi * 128:
                                          wqtr_col(buf, k) + (mi + 1) * 128]
                                rhs = AR1[:, xnf_col(k) + tb * 512:
                                          xnf_col(k) + (tb + 1) * 512]
                                return e.matmul(PS[pbk][:], lhs, rhs,
                                                start=(k == DC - 1), stop=(k == 0))
                            g.op("T", fmm,
                                 reads=[(A1, wqtr_col(buf, k), wqtr_col(buf, k) + 512),
                                        (A1, xnf_col(k) + tb * 512,
                                         xnf_col(k) + (tb + 1) * 512)],
                                 writes=[bank(pbk)])
                    for mi in mis:
                        h = a * 4 + mi
                        pbk = mi
                        ev = next_ev()
                        def fevk(e, pbk=pbk, ev=ev):
                            return e.tensor_copy(evac[:, ev * 512:(ev + 1) * 512],
                                                 PS[pbk][:])
                        g.op("V", fevk, reads=[bank(pbk)], writes=[("ev", ev, ev + 1)])
                        def fspk(e, h=h, tb=tb, ev=ev):
                            return e.dma_start(out=d_kf[h, tb],
                                               in_=evac[:, ev * 512:(ev + 1) * 512])
                        g.op("Y", fspk, reads=[("ev", ev, ev + 1)],
                             writes=[(DKF, h * 6 + tb, h * 6 + tb + 1)], dma=True)

        for t in range(CT):
            xs = t % 2
            xlo = xslot_col(xs)
            rg = t % 2
            b6lo = rg * 24
            b2lo = rg * 2
            clo = rg * 8
            def fxl(e, t=t, xlo=xlo):
                return e.dma_start(out=AR2f[:, xlo:xlo + 2048],
                                  in_=x_tm[t * 128:(t + 1) * 128, :])
            g.op("Y", fxl, writes=[(A2F, xlo, xlo + 2048)], dma=True)
            for c4 in range(4):
                def fbs(e, xlo=xlo, b6lo=b6lo, c4=c4):
                    return e.bn_stats(bn6[:, b6lo + c4 * 6:b6lo + (c4 + 1) * 6],
                                      AR2f[:, xlo + c4 * 512: xlo + (c4 + 1) * 512])
                g.op("V", fbs, reads=[(A2F, xlo, xlo + 2048)],
                     writes=[stv(rg, c4)])
            def fag(e, b6lo=b6lo, b2lo=b2lo):
                return e.bn_aggr(bn2[:, b2lo:b2lo + 2], bn6[:, b6lo:b6lo + 24])
            g.op("V", fag, reads=[stv(rg, 0), stv(rg, 1), stv(rg, 2), stv(rg, 3)],
                 writes=[stv(rg, 4)])
            def fep(e, b2lo=b2lo, clo=clo):
                return e.tensor_scalar(cols[:, clo + 2:clo + 3],
                                       bn2[:, b2lo + 1:b2lo + 2],
                                       LN_EPS, None, ALU.add)
            g.op("V", fep, reads=[stv(rg, 4)], writes=[stv(rg, 5)])
            def fsq(e, clo=clo):
                return e.activation(cols[:, clo + 2:clo + 3],
                                    cols[:, clo + 2:clo + 3], AF.Sqrt)
            g.op("S", fsq, reads=[stv(rg, 5)], writes=[stv(rg, 5)])
            def frc(e, clo=clo):
                return e.reciprocal(cols[:, clo:clo + 1], cols[:, clo + 2:clo + 3])
            g.op("V", frc, reads=[stv(rg, 5)], writes=[stv(rg, 6)])
            def fbt(e, b2lo=b2lo, clo=clo):
                return e.tensor_scalar(cols[:, clo + 1:clo + 2],
                                       bn2[:, b2lo:b2lo + 1],
                                       cols[:, clo:clo + 1], -1.0,
                                       ALU.mult, ALU.mult)
            g.op("V", fbt, reads=[stv(rg, 4), stv(rg, 6)], writes=[stv(rg, 7)])
            ns = t % 2
            nlo = xnslot_col(ns)
            def fap(e, xlo=xlo, nlo=nlo, clo=clo):
                return e.activation(AR2b[:, nlo:nlo + 2048],
                                    AR2f[:, xlo:xlo + 2048], AF.Identity,
                                    bias=cols[:, clo + 1:clo + 2],
                                    scale=cols[:, clo:clo + 1])
            g.op("S", fap, reads=[(A2F, xlo, xlo + 2048), stv(rg, 6), stv(rg, 7)],
                 writes=[(A2B, nlo, nlo + 2048)])
            for c in range(DC):
                pb = c % 2
                def ftr(e, nlo=nlo, c=c, pb=pb):
                    return e.transpose(PT[pb][:],
                                       AR2b[:, nlo + c * 128: nlo + (c + 1) * 128],
                                       ident[:])
                g.op("T", ftr, reads=[(A2B, nlo, nlo + 2048), cst("ident")],
                     writes=[bank(6 + pb)])
                dst = xnf_col(c, t)
                def fev(e, pb=pb, dst=dst):
                    return e.copy(AR1[:, dst:dst + 128], PT[pb][:])
                g.op("S", fev, reads=[bank(6 + pb)], writes=[(A1, dst, dst + 128)])

            if t >= OT0:
                tq = t - OT0
                for c4 in range(4):
                    def fbs2(e, nlo=nlo, b6lo=b6lo, c4=c4):
                        return e.bn_stats(bn6[:, b6lo + c4 * 6:b6lo + (c4 + 1) * 6],
                                          AR2b[:, nlo + c4 * 512: nlo + (c4 + 1) * 512])
                    g.op("V", fbs2, reads=[(A2B, nlo, nlo + 2048)],
                         writes=[stv(rg, c4)])
                def fag2(e, b6lo=b6lo, b2lo=b2lo):
                    return e.bn_aggr(bn2[:, b2lo:b2lo + 2], bn6[:, b6lo:b6lo + 24])
                g.op("V", fag2, reads=[stv(rg, 0), stv(rg, 1), stv(rg, 2), stv(rg, 3)],
                     writes=[stv(rg, 4)])
                def fep2(e, b2lo=b2lo, clo=clo):
                    return e.tensor_scalar(cols[:, clo + 2:clo + 3],
                                           bn2[:, b2lo + 1:b2lo + 2],
                                           RMS_EPS, None, ALU.add)
                g.op("V", fep2, reads=[stv(rg, 4)], writes=[stv(rg, 5)])
                def fsq2(e, clo=clo):
                    return e.activation(cols[:, clo + 2:clo + 3],
                                        cols[:, clo + 2:clo + 3], AF.Sqrt)
                g.op("S", fsq2, reads=[stv(rg, 5)], writes=[stv(rg, 5)])
                def frc2(e, clo=clo):
                    return e.reciprocal(cols[:, clo + 4:clo + 5],
                                        cols[:, clo + 2:clo + 3])
                g.op("V", frc2, reads=[stv(rg, 5)], writes=[stv(rg, 8)])
                def ftt(e, nlo=nlo, xlo=xlo, b2lo=b2lo):
                    return e.tensor_scalar(AR2f[:, xlo:xlo + 2048],
                                           AR2b[:, nlo:nlo + 2048],
                                           bn2[:, b2lo:b2lo + 1], None,
                                           ALU.subtract)
                g.op("V", ftt, reads=[(A2B, nlo, nlo + 2048), stv(rg, 4)],
                     writes=[(A2F, xlo, xlo + 2048)])
                def fam(e, xlo=xlo, clo=clo):
                    return e.reduce_max(cols[:, clo + 5:clo + 6],
                                        AR2f[:, xlo:xlo + 2048],
                                        axis=AX.X, apply_absolute_value=True)
                g.op("V", fam, reads=[(A2F, xlo, xlo + 2048)], writes=[stv(rg, 9)])
                def fc1(e, clo=clo):
                    return e.reciprocal(cols[:, clo + 6:clo + 7],
                                        cols[:, clo + 5:clo + 6])
                g.op("V", fc1, reads=[stv(rg, 9)], writes=[stv(rg, 10)])
                def fc1b(e, clo=clo):
                    return e.tensor_scalar(cols[:, clo + 6:clo + 7],
                                           cols[:, clo + 6:clo + 7],
                                           127.0, None, ALU.mult)
                g.op("V", fc1b, reads=[stv(rg, 10)], writes=[stv(rg, 10)])
                def fiv(e, clo=clo):
                    return e.tensor_scalar(cols[:, clo + 7:clo + 8],
                                           cols[:, clo + 5:clo + 6],
                                           cols[:, clo + 4:clo + 5], 1.0 / 127.0,
                                           ALU.mult, ALU.mult)
                g.op("V", fiv, reads=[stv(rg, 9), stv(rg, 8)], writes=[stv(rg, 11)])
                def fqz(e, xlo=xlo, clo=clo):
                    return e.tensor_scalar(SCRI[:], AR2f[:, xlo:xlo + 2048],
                                           cols[:, clo + 6:clo + 7], None,
                                           ALU.mult)
                g.op("V", fqz, reads=[(A2F, xlo, xlo + 2048), stv(rg, 10)],
                     writes=[("scri", 0, 1)])
                def fqq(e, nlo=nlo, clo=clo):
                    return e.tensor_scalar(AR2b[:, nlo:nlo + 2048], SCRI[:],
                                           cols[:, clo + 7:clo + 8], None,
                                           ALU.mult)
                g.op("V", fqq, reads=[("scri", 0, 1), stv(rg, 11)],
                     writes=[(A2B, nlo, nlo + 2048)])
                for c in range(DC):
                    pb = c % 2
                    def ftr2(e, nlo=nlo, c=c, pb=pb):
                        return e.transpose(PT[pb][:],
                                           AR2b[:, nlo + c * 128: nlo + (c + 1) * 128],
                                           ident[:])
                    g.op("T", ftr2, reads=[(A2B, nlo, nlo + 2048), cst("ident")],
                         writes=[bank(6 + pb)])
                    dst = c * T + tq * 128
                    def fev2(e, pb=pb, dst=dst):
                        return e.tensor_copy(XQ[:, dst:dst + 128], PT[pb][:])
                    g.op("V", fev2, reads=[bank(6 + pb)], writes=[(AXQ, dst, dst + 128)])
            emit_pw(2)
            if t % 4 == 3:
                emit_kproj_block(t // 4, [0, 1], {0: 0, 1: 1})

        emit_qkvo_quarter_load(wk, 2, 0)
        emit_qkvo_quarter_load(wk, 3, 1)
        for tb in range(6):
            emit_kproj_block(tb, [2, 3], {2: 0, 3: 1})
            emit_pw(6)

        def proj(kind, w_ap, skip_quarters):
            for a in range(4):
                buf = a % 2
                if (kind, a) not in skip_quarters:
                    emit_qkvo_quarter_load(w_ap, a, buf)
                if kind == "k":
                    for mi in range(4):
                        h = a * 4 + mi
                        for tb in range(6):
                            pbk = (h * 6 + tb) % 4
                            for k in range(DC - 1, -1, -1):
                                def fmm(e, buf=buf, mi=mi, k=k, tb=tb, pbk=pbk):
                                    lhs = AR1[:, wqtr_col(buf, k) + mi * 128:
                                              wqtr_col(buf, k) + (mi + 1) * 128]
                                    rhs = AR1[:, xnf_col(k) + tb * 512:
                                              xnf_col(k) + (tb + 1) * 512]
                                    return e.matmul(PS[pbk][:], lhs, rhs,
                                                    start=(k == DC - 1), stop=(k == 0))
                                g.op("T", fmm,
                                     reads=[(A1, wqtr_col(buf, k),
                                             wqtr_col(buf, k) + 512),
                                            (A1, xnf_col(k) + tb * 512,
                                             xnf_col(k) + (tb + 1) * 512)],
                                     writes=[bank(pbk)])
                            ev = (h * 6 + tb) % EVN
                            def fevk(e, pbk=pbk, ev=ev):
                                return e.tensor_copy(evac[:, ev * 512:(ev + 1) * 512],
                                                     PS[pbk][:])
                            g.op("V", fevk, reads=[bank(pbk)],
                                 writes=[("ev", ev, ev + 1)])
                            def fspk(e, h=h, tb=tb, ev=ev):
                                return e.dma_start(out=d_kf[h, tb],
                                                   in_=evac[:, ev * 512:(ev + 1) * 512])
                            g.op("Y", fspk, reads=[("ev", ev, ev + 1)],
                                 writes=[(DKF, h * 6 + tb, h * 6 + tb + 1)], dma=True)
                elif kind == "v":
                    for tp_ in range(CT // 2):
                        tts = (2 * tp_, 2 * tp_ + 1)
                        for k in range(DC - 1, -1, -1):
                            for tt in tts:
                                pbk = tt % 4
                                def fmv(e, buf=buf, tt=tt, k=k, pbk=pbk):
                                    lhs = AR1[:, xnf_col(k) + tt * 128:
                                              xnf_col(k) + (tt + 1) * 128]
                                    rhs = AR1[:, wqtr_col(buf, k):wqtr_col(buf, k) + 512]
                                    return e.matmul(PS[pbk][:], lhs, rhs,
                                                    start=(k == DC - 1), stop=(k == 0))
                                g.op("T", fmv,
                                     reads=[(A1, wqtr_col(buf, k), wqtr_col(buf, k) + 512),
                                            (A1, xnf_col(k) + tt * 128,
                                             xnf_col(k) + (tt + 1) * 128)],
                                     writes=[bank(pbk)])
                        for tt in tts:
                            pbk = tt % 4
                            ev = next_ev()
                            def fevv(e, pbk=pbk, ev=ev):
                                return e.tensor_copy(evac[:, ev * 512:(ev + 1) * 512],
                                                     PS[pbk][:])
                            g.op("V", fevv, reads=[bank(pbk)], writes=[("ev", ev, ev + 1)])
                            def fspv(e, tt=tt, a=a, ev=ev):
                                return e.dma_start(out=d_vtm[a, tt],
                                                   in_=evac[:, ev * 512:(ev + 1) * 512])
                            g.op("Y", fspv, reads=[("ev", ev, ev + 1)],
                                 writes=[(DVT, a * CT + tt, a * CT + tt + 1)], dma=True)
                else:
                    for mi in range(4):
                        h = a * 4 + mi
                        for k in range(DC - 1, -1, -1):
                            for qb in range(QB):
                                tb = 4 + qb
                                pbk = (h * 2 + qb) % 4
                                def fmq(e, buf=buf, mi=mi, k=k, tb=tb, pbk=pbk):
                                    lhs = AR1[:, wqtr_col(buf, k) + mi * 128:
                                              wqtr_col(buf, k) + (mi + 1) * 128]
                                    rhs = AR1[:, xnf_col(k) + tb * 512:
                                              xnf_col(k) + (tb + 1) * 512]
                                    return e.matmul(PS[pbk][:], lhs, rhs,
                                                    start=(k == DC - 1), stop=(k == 0))
                                g.op("T", fmq,
                                     reads=[(A1, wqtr_col(buf, k),
                                             wqtr_col(buf, k) + 512),
                                            (A1, xnf_col(k) + tb * 512,
                                             xnf_col(k) + (tb + 1) * 512)],
                                     writes=[bank(pbk)])
                        for qb in range(QB):
                            pbk = (h * 2 + qb) % 4
                            ev = next_ev()
                            def fevq(e, pbk=pbk, ev=ev):
                                return e.tensor_copy(evac[:, ev * 512:(ev + 1) * 512],
                                                     PS[pbk][:])
                            g.op("V", fevq, reads=[bank(pbk)],
                                 writes=[("ev", ev, ev + 1)])
                            def fspq(e, h=h, qb=qb, ev=ev):
                                return e.dma_start(out=d_qf[h, qb],
                                                   in_=evac[:, ev * 512:(ev + 1) * 512])
                            g.op("Y", fspq, reads=[("ev", ev, ev + 1)],
                                 writes=[(DQF, h * 2 + qb, h * 2 + qb + 1)], dma=True)
                emit_pw(18)

        proj("v", wv, set())
        proj("q", wq, set())
        emit_pw(len(pw_jobs))

        for hc in range(FC):
            sg_ = hc % 2
            def flg(e, hc=hc, sg_=sg_):
                return e.dma_start(out=AR2b[:, wstage_col(sg_):wstage_col(sg_) + 2048],
                                   in_=d_wgs[hc])
            g.op("G", flg, reads=[(DWG, hc, hc + 1)],
                 writes=[(A2B, wstage_col(sg_), wstage_col(sg_) + 2048)], dma=True)
            def flu(e, hc=hc, sg_=sg_):
                return e.dma_start(
                    out=AR2b[:, wstage_col(2 + sg_):wstage_col(2 + sg_) + 2048],
                    in_=d_wus[hc])
            g.op("G", flu, reads=[(DWU, hc, hc + 1)],
                 writes=[(A2B, wstage_col(2 + sg_), wstage_col(2 + sg_) + 2048)],
                 dma=True)
            for qb in range(QB):
                bg = (hc * 4 + qb * 2) % 6
                bu = (hc * 4 + qb * 2 + 1) % 6
                for k in range(DC - 1, -1, -1):
                    def fmg(e, sg_=sg_, k=k, qb=qb, bg=bg):
                        lhs = AR2b[:, wstage_col(sg_) + k * 128:
                                   wstage_col(sg_) + (k + 1) * 128]
                        rhs = XQ[:, k * T + qb * 512: k * T + (qb + 1) * 512]
                        return e.matmul(PS[bg][:], lhs, rhs,
                                        start=(k == DC - 1), stop=(k == 0))
                    g.op("T", fmg,
                         reads=[(A2B, wstage_col(sg_) + k * 128,
                                 wstage_col(sg_) + (k + 1) * 128),
                                (AXQ, k * T + qb * 512, k * T + (qb + 1) * 512)],
                         writes=[bank(bg)])
                    def fmu(e, sg_=sg_, k=k, qb=qb, bu=bu):
                        lhs = AR2b[:, wstage_col(2 + sg_) + k * 128:
                                   wstage_col(2 + sg_) + (k + 1) * 128]
                        rhs = XQ[:, k * T + qb * 512: k * T + (qb + 1) * 512]
                        return e.matmul(PS[bu][:], lhs, rhs,
                                        start=(k == DC - 1), stop=(k == 0))
                    g.op("T", fmu,
                         reads=[(A2B, wstage_col(2 + sg_) + k * 128,
                                 wstage_col(2 + sg_) + (k + 1) * 128),
                                (AXQ, k * T + qb * 512, k * T + (qb + 1) * 512)],
                         writes=[bank(bu)])
                sslot = P4_SG + ((hc * 2 + qb) % 4) * 512
                def fsg(e, bg=bg, sslot=sslot):
                    return e.activation(AR2f[:, sslot:sslot + 512], PS[bg][:], AF.Silu)
                g.op("S", fsg, reads=[bank(bg)], writes=[(A2F, sslot, sslot + 512)])
                dst = p_col(hc, qb * 512)
                def fgate(e, bu=bu, sslot=sslot, dst=dst):
                    return e.tensor_tensor(AR1[:, dst:dst + 512], PS[bu][:],
                                           AR2f[:, sslot:sslot + 512], ALU.mult)
                g.op("V", fgate, reads=[bank(bu), (A2F, sslot, sslot + 512)],
                     writes=[(A1, dst, dst + 512)])

        for mb in range(4):
            for qth in range(2):
                for hc8 in range(FC // 8):
                    ws = hc8 % 2
                    wlo = ws * 4096
                    def fld(e, hc8=hc8, mb=mb, wlo=wlo):
                        src = d_wds[hc8 * 8:(hc8 + 1) * 8, :, mb * 512:(mb + 1) * 512]
                        src = src.rearrange("c p n -> p c n")
                        dst = AR2b[:, wlo:wlo + 4096].rearrange(
                            "p (c n) -> p c n", c=8)
                        return e.dma_start(out=dst, in_=src)
                    g.op("S", fld, reads=[(DWD, hc8 * 8, hc8 * 8 + 8)],
                         writes=[(A2B, wlo, wlo + 4096)], dma=True)
                    for ci in range(7, -1, -1):
                        hc = hc8 * 8 + ci
                        for qi in range(4):
                            qt = qth * 4 + qi
                            def fmd(e, hc=hc, qt=qt, wlo=wlo, ci=ci, qi=qi):
                                lhs = AR1[:, p_col(hc, qt * 128):
                                          p_col(hc, qt * 128) + 128]
                                rhs = AR2b[:, wlo + ci * 512: wlo + (ci + 1) * 512]
                                return e.matmul(PS[qi][:], lhs, rhs,
                                                start=(hc == 7), stop=(hc == FC - 8))
                            g.op("T", fmd,
                                 reads=[(A1, p_col(hc, qt * 128),
                                         p_col(hc, qt * 128) + 128),
                                        (A2B, wlo + ci * 512, wlo + (ci + 1) * 512)],
                                 writes=[bank(qi)])
                for qi in range(4):
                    qt = qth * 4 + qi
                    gdst = qt * 2048 + mb * 512
                    def fsl(e, qi=qi, gdst=gdst):
                        return e.activation(XQ[:, gdst:gdst + 512], PS[qi][:],
                                            AF.Silu)
                    g.op("S", fsl, reads=[bank(qi)],
                         writes=[(AXQ, gdst, gdst + 512)])

        def fml(e):
            return e.dma_start(out=AR2b[:, MASKS_COL:MASKS_COL + 4096], in_=masks_in)
        g.op("S", fml, writes=[(A2B, MASKS_COL, MASKS_COL + 4096)], dma=True)

        MASK_SLOTS = {0: 0, 1: 1, 2: 2, 3: 3, 16: 4, 17: 5, 18: 6, 19: 7}

        def load_group(grp, gb):
            h0 = grp * 4
            for hl in range(4):
                dst = kbuf_col(gb, hl)
                def fkl(e, h0=h0, hl=hl, dst=dst):
                    src = d_kf[h0 + hl].rearrange("t p n -> p t n")
                    dsta = AR1[:, dst:dst + 3072].rearrange("p (t n) -> p t n", t=6)
                    return e.dma_start(out=dsta, in_=src)
                g.op("G", fkl, reads=[(DKF, (h0 + hl) * 6, (h0 + hl) * 6 + 6)],
                     writes=[(A1, dst, dst + 3072)], dma=True)
            for vq in range(4):
                dst = vbuf_col(gb, vq * 6)
                def fvl(e, grp=grp, vq=vq, dst=dst):
                    src = d_vtm[grp, vq * 6:(vq + 1) * 6].rearrange("t p n -> p t n")
                    dsta = AR1[:, dst:dst + 3072].rearrange("p (t n) -> p t n", t=6)
                    return e.dma_start(out=dsta, in_=src)
                g.op("G", fvl, reads=[(DVT, grp * CT + vq * 6,
                                       grp * CT + (vq + 1) * 6)],
                     writes=[(A1, dst, dst + 3072)], dma=True)

        def load_q(h, qb, slot):
            dst = qring_col(slot)
            def fql(e, h=h, qb=qb, dst=dst):
                return e.dma_start(out=AR2b[:, dst:dst + 512], in_=d_qf[h, qb])
            g.op("S", fql, reads=[(DQF, h * 2 + qb, h * 2 + qb + 1)],
                 writes=[(A2B, dst, dst + 512)], dma=True)

        load_group(0, 0)
        load_q(0, 0, 0)
        qtoggle = 0
        for grp in range(4):
            gb = grp % 2
            if grp < 3:
                load_group(grp + 1, (grp + 1) % 2)
            for hl in range(4):
                h = grp * 4 + hl
                for qb in range(QB):
                    qslot = qtoggle
                    nqh, nqb = (h, qb + 1) if qb + 1 < QB else (h + 1, 0)
                    if nqh < H:
                        load_q(nqh, nqb, qtoggle ^ 1)
                    qtoggle ^= 1
                    def emit_s(j):
                        sb = j % 2
                        ktloc = qb * 512 + 128 * j
                        kcol = kbuf_col(gb, hl) + ktloc
                        def fms(e, kcol=kcol, qslot=qslot, sb=sb):
                            return e.matmul(
                                PS[sb][:], AR1[:, kcol:kcol + 128],
                                AR2b[:, qring_col(qslot):qring_col(qslot) + 512],
                                start=True, stop=True)
                        g.op("T", fms, reads=[(A1, kcol, kcol + 128),
                                              (A2B, qring_col(qslot),
                                               qring_col(qslot) + 512)],
                             writes=[bank(sb)])
                        ptl = ptil_col(j % 4)
                        bcol = qb * J + j
                        def fex(e, sb=sb, ptl=ptl, bcol=bcol):
                            return e.activation(AR2b[:, ptl:ptl + 512], PS[sb][:],
                                                AF.Exp, bias=padb[:, bcol:bcol + 1],
                                                scale=SCALE)
                        g.op("S", fex, reads=[bank(sb), cst("padb")],
                             writes=[(A2B, ptl, ptl + 512)])
                        if j in MASK_SLOTS:
                            mcol = MASKS_COL + MASK_SLOTS[j] * 512
                            def fmk(e, ptl=ptl, mcol=mcol):
                                return e.tensor_tensor(AR2b[:, ptl:ptl + 512],
                                                       AR2b[:, ptl:ptl + 512],
                                                       AR2b[:, mcol:mcol + 512],
                                                       ALU.mult)
                            g.op("V", fmk, reads=[(A2B, ptl, ptl + 512),
                                                  (A2B, mcol, mcol + 512)],
                                 writes=[(A2B, ptl, ptl + 512)])
                        if j == 0:
                            def fda(e, ptl=ptl):
                                return e.tensor_copy(AR2b[:, SDEN:SDEN + 512],
                                                     AR2b[:, ptl:ptl + 512])
                        else:
                            def fda(e, ptl=ptl):
                                return e.tensor_tensor(AR2b[:, SDEN:SDEN + 512],
                                                       AR2b[:, SDEN:SDEN + 512],
                                                       AR2b[:, ptl:ptl + 512],
                                                       ALU.add)
                        g.op("V", fda, reads=[(A2B, ptl, ptl + 512),
                                              (A2B, SDEN, SDEN + 512)],
                             writes=[(A2B, SDEN, SDEN + 512)])
                    def emit_av(j):
                        ktloc = qb * 512 + 128 * j
                        ptl = ptil_col(j % 4)
                        ttl = ktloc // 128
                        vcol = vbuf_col(gb, ttl) + hl * 128
                        cbank = 2 + (j % 2)
                        def fav(e, vcol=vcol, ptl=ptl, cbank=cbank, j=j):
                            return e.matmul(PS[cbank][:], AR1[:, vcol:vcol + 128],
                                            AR2b[:, ptl:ptl + 512],
                                            start=(j < 2), stop=(j >= J - 2))
                        g.op("T", fav, reads=[(A1, vcol, vcol + 128),
                                              (A2B, ptl, ptl + 512)],
                             writes=[bank(cbank)])
                    emit_s(0)
                    emit_s(1)
                    for j in range(J):
                        if j + 2 < J:
                            emit_s(j + 2)
                        emit_av(j)
                    def fdn(e):
                        return e.matmul(PS[4][0:1, :], onesc[:],
                                        AR2b[:, SDEN:SDEN + 512],
                                        start=True, stop=True)
                    g.op("T", fdn, reads=[cst("onesc"), (A2B, SDEN, SDEN + 512)],
                         writes=[bank(4)])
                    g.op("V", lambda e: e.reciprocal(linv[:], PS[4][0:1, :]),
                         reads=[bank(4)], writes=[cst("linv")])
                    def fbc(e):
                        return e.matmul(PS[5][:], onesr[:], linv[:],
                                        start=True, stop=True)
                    g.op("T", fbc, reads=[cst("onesr"), cst("linv")],
                         writes=[bank(5)])
                    g.op("V", lambda e: e.tensor_copy(linvb[:], PS[5][:]),
                         reads=[bank(5)], writes=[cst("linvb")])
                    def fsum0(e):
                        return e.tensor_copy(AR2b[:, SDEN + 512:SDEN + 1024],
                                             PS[2][:])
                    g.op("V", fsum0, reads=[bank(2)],
                         writes=[(A2B, SDEN + 512, SDEN + 1024)])
                    def fsum(e):
                        return e.tensor_tensor(AR2b[:, SDEN + 512:SDEN + 1024],
                                               AR2b[:, SDEN + 512:SDEN + 1024],
                                               PS[3][:], ALU.add)
                    g.op("V", fsum, reads=[(A2B, SDEN + 512, SDEN + 1024), bank(3)],
                         writes=[(A2B, SDEN + 512, SDEN + 1024)])
                    dst = ctx_col(h, qb * 512)
                    def fdv(e, dst=dst):
                        return e.tensor_tensor(AR1[:, dst:dst + 512],
                                               AR2b[:, SDEN + 512:SDEN + 1024],
                                               linvb[:], ALU.mult)
                    g.op("V", fdv, reads=[(A2B, SDEN + 512, SDEN + 1024),
                                          cst("linvb")],
                         writes=[(A1, dst, dst + 512)])

        for (k0, k1) in ((0, 12), (12, 16)):
            for a in range(4):
                def fwo(e, a=a, k0=k0, k1=k1):
                    src = wo[k0 * 128:k1 * 128,
                             a * 512:(a + 1) * 512].rearrange(
                        "(k p) n -> p k n", p=128)
                    base = 16384 + k0 * 2048 + a * 512
                    dst = AR1[:, base:base + (k1 - k0) * 2048]
                    dst = dst.rearrange("p (k n) -> p k n",
                                        k=k1 - k0, n=2048)[:, :, 0:512]
                    return e.dma_start(out=dst, in_=src)
                g.op("G", fwo,
                     writes=[(A1, 16384 + k0 * 2048 + a * 512,
                              16384 + (k1 - 1) * 2048 + a * 512 + 512)],
                     dma=True)

        for qt in range(8):
            xlo = xslot_col(qt % 2)
            def fx3(e, qt=qt, xlo=xlo):
                return e.dma_start(out=AR2f[:, xlo:xlo + 2048],
                                  in_=x_tm[2048 + qt * 128: 2048 + (qt + 1) * 128, :])
            g.op("S" if qt % 2 == 0 else "G", fx3,
                 writes=[(A2F, xlo, xlo + 2048)], dma=True)
            for mp in range(2):
                mbs = (2 * mp, 2 * mp + 1)
                for hh in list(range(11, -1, -1)) + [15, 14, 13, 12]:
                    for mb in mbs:
                        def fwo2(e, hh=hh, qt=qt, mb=mb, bk=mb):
                            lhs = AR1[:, ctx_col(hh, qt * 128):ctx_col(hh, qt * 128) + 128]
                            rhs = AR1[:, wo_col(hh) + mb * 512:
                                      wo_col(hh) + (mb + 1) * 512]
                            return e.matmul(PS[bk][:], lhs, rhs,
                                            start=(hh == 11), stop=(hh == 12))
                        g.op("T", fwo2,
                             reads=[(A1, ctx_col(hh, qt * 128),
                                     ctx_col(hh, qt * 128) + 128),
                                    (A1, wo_col(hh) + mb * 512,
                                     wo_col(hh) + (mb + 1) * 512)],
                             writes=[bank(mb)])
                for mb in mbs:
                    gsrc = qt * 2048 + mb * 512
                    tcol = 4096 + mb * 512
                    def fc1(e, bk=mb, gsrc=gsrc, tcol=tcol):
                        return e.tensor_tensor(
                            AR2b[:, tcol:tcol + 512], PS[bk][:],
                            XQ[:, gsrc:gsrc + 512], ALU.mult)
                    g.op("V", fc1, reads=[bank(mb), (AXQ, gsrc, gsrc + 512)],
                         writes=[(A2B, tcol, tcol + 512)])
                    def fc2(e, mb=mb, tcol=tcol, xlo=xlo):
                        return e.tensor_tensor(
                            AR2f[:, xlo + mb * 512: xlo + (mb + 1) * 512],
                            AR2f[:, xlo + mb * 512: xlo + (mb + 1) * 512],
                            AR2b[:, tcol:tcol + 512], ALU.add)
                    g.op("V", fc2, reads=[(A2F, xlo + mb * 512, xlo + (mb + 1) * 512),
                                          (A2B, tcol, tcol + 512)],
                         writes=[(A2F, xlo + mb * 512, xlo + (mb + 1) * 512)])
            def fst(e, qt=qt, xlo=xlo):
                return e.dma_start(out=out_hbm[qt * 128:(qt + 1) * 128, :],
                                   in_=AR2f[:, xlo:xlo + 2048])
            g.op("Y", fst, reads=[(A2F, xlo, xlo + 2048)],
                 writes=[("outhbm", qt, qt + 1)], dma=True)

        g.barrier_wait("Y", [(f"Y{i}", g.cnt[f"Y{i}"]) for i in range(NSLOT)
                             if g.cnt[f"Y{i}"] > 0])

        @block.tensor
        def _(e):
            for th in g.thunks["T"]:
                th(e, sems)
        @block.vector
        def _(e):
            for th in g.thunks["V"]:
                th(e, sems)
        @block.scalar
        def _(e):
            for th in g.thunks["S"]:
                th(e, sems)
        @block.sync
        def _(e):
            for th in g.thunks["Y"]:
                th(e, sems)
        @block.gpsimd
        def _(e):
            for th in g.thunks["G"]:
                th(e, sems)

    return nc


_NC_CACHE = None

def _get_nc():
    global _NC_CACHE
    if _NC_CACHE is None:
        _NC_CACHE = build_program()
    return _NC_CACHE


def _host_inputs(L0_attn_x, L0_attn_Wq, L0_attn_Wk, L0_attn_Wv, L0_attn_Wo,
                 L0_ffn_W_g, L0_ffn_W_u, L0_ffn_W_d):
    x = np.ascontiguousarray(np.asarray(L0_attn_x, dtype=np.float32))
    wq = np.ascontiguousarray(np.asarray(L0_attn_Wq, dtype=np.float32))
    wk = np.ascontiguousarray(np.asarray(L0_attn_Wk, dtype=np.float32))
    wv = np.ascontiguousarray(np.asarray(L0_attn_Wv, dtype=np.float32))
    wo = np.ascontiguousarray(np.asarray(L0_attn_Wo, dtype=np.float32))
    wgT = np.ascontiguousarray(np.asarray(L0_ffn_W_g, dtype=np.float32).T)
    wuT = np.ascontiguousarray(np.asarray(L0_ffn_W_u, dtype=np.float32).T)
    wdT = np.ascontiguousarray(np.asarray(L0_ffn_W_d, dtype=np.float32).T)

    p = np.arange(128)[:, None]
    jj = np.arange(512)[None, :]
    masks = np.zeros((128, 8 * 512), dtype=np.float32)
    for j in range(4):
        masks[:, j * 512:(j + 1) * 512] = (jj < p + 128 * j)
    for idx, j in enumerate(range(4)):
        masks[:, (4 + idx) * 512:(5 + idx) * 512] = (jj >= p + 128 * j)
    masks8 = masks.astype(BFNP)

    ident = np.eye(128, dtype=np.float32).astype(BFNP)
    onesc = np.ones((128, 1), dtype=np.float32).astype(BFNP)
    onesr = np.ones((1, 128), dtype=np.float32)

    in_maps = []
    for c in range(NCORES):
        b, i = divmod(c, 4)
        own0 = T * i
        lo = own0 - 2048
        x_tm_c = np.zeros((C, D), dtype=np.float32)
        src_lo = max(lo, 0)
        x_tm_c[src_lo - lo: C] = x[b, src_lo: own0 + T]
        padbias = np.zeros((128, 2 * J), dtype=np.float32)
        for qb in range(QB):
            for j in range(J):
                if own0 + 512 * qb - 2048 + 128 * j < 0:
                    padbias[:, qb * J + j] = PAD_BIAS
        in_maps.append({
            "x_tm": x_tm_c, "wq": wq, "wk": wk, "wv": wv, "wo": wo,
            "wgT": wgT, "wuT": wuT, "wdT": wdT,
            "masks8": masks8, "padbias": padbias,
            "ident": ident, "onesc": onesc, "onesr": onesr,
        })
    return in_maps


def kernel(L0_attn_x, L0_attn_Wq, L0_attn_Wk, L0_attn_Wv, L0_attn_Wo,
           L0_ffn_W_g, L0_ffn_W_u, L0_ffn_W_d):
    in_maps = _host_inputs(L0_attn_x, L0_attn_Wq, L0_attn_Wk, L0_attn_Wv,
                           L0_attn_Wo, L0_ffn_W_g, L0_ffn_W_u, L0_ffn_W_d)
    nc = _get_nc()
    res = run_bass_kernel_spmd(nc, in_maps, core_ids=list(range(NCORES)))
    full = np.empty((2, 4096, D), dtype=np.float32)
    for c in range(NCORES):
        b, i = divmod(c, 4)
        full[b, T * i: T * (i + 1)] = np.asarray(res.results[c]["out"])
    return full
